# version 11
# baseline (speedup 1.0000x reference)
import contextlib

import numpy as np

import concourse.bass as bass
import concourse.mybir as mybir
from concourse.bass_utils import run_bass_kernel_spmd

F32 = mybir.dt.float32
BF16 = mybir.dt.bfloat16
AF = mybir.ActivationFunctionType
ALU = mybir.AluOpType
AX = mybir.AxisListType

D = 1024
NS = 16
KCONV = 4
DH = 4096
NE = 8
NJ = 8
LC = 256
SC = 512
SG = 4
LN_EPS = 1e-5

ENGINES = ("sync", "tensor", "vector", "scalar", "gpsimd")


class Ctr:
    def __init__(self, st, nc, name):
        self.sem = st.enter_context(nc.semaphore(name))
        self.v = 0


def build_nc(T, detect_races=True, dbg=False):
    nchunk = T // LC
    npass = T // SC
    nc = bass.Bass("TRN2", target_bir_lowering=False,
                   detect_race_conditions=detect_races)

    dx = nc.dram_tensor("x", [T, D], F32, kind="ExternalInput").ap()
    d_ln_g = nc.dram_tensor("ln_g_fm", [128, NJ], F32, kind="ExternalInput").ap()
    d_ln_b = nc.dram_tensor("ln_b_fm", [128, NJ], F32, kind="ExternalInput").ap()
    d_Win = nc.dram_tensor("W_in", [D, 2 * D], F32, kind="ExternalInput").ap()
    d_convw = nc.dram_tensor("conv_w_fm", [128, NJ * KCONV], F32, kind="ExternalInput").ap()
    d_convb = nc.dram_tensor("conv_b_fm", [128, NJ], F32, kind="ExternalInput").ap()
    d_dtw = nc.dram_tensor("dt_w", [D, D], F32, kind="ExternalInput").ap()
    d_dtb = nc.dram_tensor("dt_b_fm", [128, NJ], F32, kind="ExternalInput").ap()
    d_Bw = nc.dram_tensor("B_w", [D, NS], F32, kind="ExternalInput").ap()
    d_Cw = nc.dram_tensor("C_w", [D, NS], F32, kind="ExternalInput").ap()
    d_logA = nc.dram_tensor("log_A_fm", [128, NJ], F32, kind="ExternalInput").ap()
    d_Dp = nc.dram_tensor("D_fm", [128, NJ], F32, kind="ExternalInput").ap()
    d_Wout = nc.dram_tensor("W_out", [D, D], F32, kind="ExternalInput").ap()
    d_Wg = nc.dram_tensor("Wg", [D, NE], F32, kind="ExternalInput").ap()
    d_bg = nc.dram_tensor("bg_row", [1, NE], F32, kind="ExternalInput").ap()
    d_W1 = nc.dram_tensor("W1", [NE, D, DH], F32, kind="ExternalInput").ap()
    d_b1 = nc.dram_tensor("b1_fm", [128, NE * 32], F32, kind="ExternalInput").ap()
    d_W2 = nc.dram_tensor("W2", [NE, DH, D], F32, kind="ExternalInput").ap()
    d_b2 = nc.dram_tensor("b2", [NE, D], F32, kind="ExternalInput").ap()
    d_ident = nc.dram_tensor("ident", [128, 128], F32, kind="ExternalInput").ap()
    d_ones_col = nc.dram_tensor("ones_col", [128, 1], F32, kind="ExternalInput").ap()
    d_ones_row = nc.dram_tensor("ones_row", [1, 128], F32, kind="ExternalInput").ap()
    d_sel16 = nc.dram_tensor("sel16", [NS, NS * 128], F32, kind="ExternalInput").ap()
    d_sel8 = nc.dram_tensor("sel8", [NE, NE * 128], F32, kind="ExternalInput").ap()

    d_out = nc.dram_tensor("out", [T, D], F32, kind="ExternalOutput").ap()

    s_Win = nc.dram_tensor("s_Win", [16, 128, 8 * 128], BF16, kind="Internal").ap()
    s_dtw = nc.dram_tensor("s_dtw", [8, 128, 8 * 128], BF16, kind="Internal").ap()
    s_Wout = nc.dram_tensor("s_Wout", [8, 128, 8 * 128], BF16, kind="Internal").ap()
    s_W1 = nc.dram_tensor("s_W1", [NE, 16, 128, 8 * 256], BF16, kind="Internal").ap()
    s_W2 = nc.dram_tensor("s_W2", [NE, 8, 128, 32 * 128], BF16, kind="Internal").ap()

    st = contextlib.ExitStack()
    with st:
        def sb(name, shape, dt):
            return st.enter_context(nc.sbuf_tensor(name, shape, dt))

        ident_f = sb("ident_f", [128, 128], F32)
        ones_col_f = sb("ones_col_f", [128, 1], F32)
        ones_col_b = sb("ones_col_b", [128, 1], BF16)
        ones_row_f = sb("ones_row_f", [1, 128], F32)
        sel16_b = sb("sel16_b", [NS, NS * 128], BF16)
        sel8_b = sb("sel8_b", [NE, NE * 128], BF16)
        ln_g = sb("ln_g", [128, NJ], F32)
        ln_b = sb("ln_b", [128, NJ], F32)
        conv_w = sb("conv_w", [128, NJ * KCONV], F32)
        conv_b = sb("conv_b", [128, NJ], F32)
        dt_b = sb("dt_b", [128, NJ], F32)
        A_neg = sb("A_neg", [128, NJ], F32)
        D_fm = sb("D_fm_s", [128, NJ], F32)
        b1_fm = sb("b1_fm_s", [128, NE * 32], F32)
        b2_b = sb("b2_b", [NE, D], BF16)
        bg_row_s = sb("bg_row_s", [1, NE], F32)
        bg_bcast = sb("bg_bcast", [128, NE], F32)
        Wg_sb = sb("Wg_sb", [128, 8 * NE], F32)
        Bw_b = sb("Bw_b", [128, 8 * NS], BF16)
        Cw_b = sb("Cw_b", [128, 8 * NS], BF16)
        h_state = sb("h_state", [128, NJ * NS], F32)
        hist = sb("hist", [128, NJ * 3], F32)
        pre = sb("pre", [128, LC + 3], F32)
        Btm = sb("Btm", [128, 2 * NS], F32)
        Ctm = sb("Ctm", [128, 2 * NS], F32)
        Bfm_b = sb("Bfm_b", [NS, LC], BF16)
        Cfm_b = sb("Cfm_b", [NS, LC], BF16)
        mu_row = sb("mu_row", [1, SC], F32)
        var_row = sb("var_row", [1, SC], F32)
        rstd_row = sb("rstd_row", [1, SC], F32)
        tmp_row = sb("tmp_row", [1, SC], F32)
        junk = sb("junk", [128, 256], F32)

        fw = sb("fw", [128, 3 * 1024], BF16)
        w1r = sb("w1r", [128, 3 * 2048], BF16)
        w2r = sb("w2r", [128, 2 * 2048], BF16)

        x_tm = sb("x_tm", [128, 2 * 1024], F32)
        out_tm = sb("out_tm", [128, 1024], F32)
        x_fm = sb("x_fm", [128, NJ * LC], F32)
        xn_b = sb("xn_b", [128, NJ * LC], BF16)
        xsq_b = sb("xsq_b", [128, NJ * LC], BF16)
        z_silu = sb("z_silu", [128, NJ * LC], BF16)
        x_ssm = sb("x_ssm", [128, NJ * LC], F32)
        x_ssm_b = sb("x_ssm_b", [128, NJ * LC], BF16)
        u_buf = sb("u_buf", [128, NJ * LC], BF16)
        B_sb = sb("B_sb", [128, SG * LC], BF16)
        C_sb = sb("C_sb", [128, SG * LC], BF16)
        decay = sb("decay", [128, NJ * LC], F32)
        dt_j = sb("dt_j", [128, LC], F32)
        sp1 = sb("sp1", [128, LC], F32)
        sp2 = sb("sp2", [128, LC], F32)
        d1s = sb("d1s", [128, LC], F32)
        h_traj = sb("h_traj", [128, SG * LC], F32)
        hc = sb("hc", [128, SG * LC], F32)
        y_buf = sb("y_buf", [128, NJ * LC], F32)

        x2 = sb("x2", [128, NJ * SC], F32)
        x2b = sb("x2b", [128, NJ * SC], BF16)
        h1 = sb("h1", [128, 32 * SC], BF16)
        o_acc = sb("o_acc", [128, NJ * SC], F32)
        tmp_o = sb("tmp_o", [128, SC], F32)
        mask_b = sb("mask_b", [128, SC], F32)
        gate_tm = sb("gate_tm", [128, 4 * NE], F32)
        probs = sb("probs", [128, 4 * NE], F32)
        mask_tm = sb("mask_tm", [128, 4 * NE], F32)
        gsc = sb("gsc", [128, 8], F32)
        w_tm = sb("w_tm", [128, 4], F32)
        maskw_fm_b = sb("maskw_fm_b", [NE, SC], BF16)

        pb = [st.enter_context(nc.psum_tensor(f"pb{i}", [128, 512], F32))
              for i in range(8)]

        cSM = Ctr(st, nc, "cSM")
        cXs = [Ctr(st, nc, f"cX{i}") for i in range(2)]
        cFWs = [Ctr(st, nc, f"cFW{i}") for i in range(3)]
        cW1s = [Ctr(st, nc, f"cW1{i}") for i in range(3)]
        cW2s = [Ctr(st, nc, f"cW2{i}") for i in range(2)]
        cCVsm = Ctr(st, nc, "cCVsm")
        cCVF = Ctr(st, nc, "cCVF")
        cCVe = [Ctr(st, nc, f"cCVe{i}") for i in range(NE)]
        cOUT = Ctr(st, nc, "cOUT")
        cDBG = Ctr(st, nc, "cDBG")
        ePE = Ctr(st, nc, "ePE")
        ePool = Ctr(st, nc, "ePool")
        eDVE = Ctr(st, nc, "eDVE")
        eACT = Ctr(st, nc, "eACT")

        prog = {e: [] for e in ENGINES}

        def em(eng, fn):
            prog[eng].append(fn)

        def wait(eng, ctr, val=None):
            if isinstance(ctr, tuple):
                sem, v = ctr
            else:
                sem, v = ctr.sem, (ctr.v if val is None else val)
            if v > 0:
                em(eng, lambda e, s=sem, v=v: e.wait_ge(s, v))

        def op(eng, fn, ctr=None, n=1):
            if ctr is None or n is None:
                em(eng, fn)
                return None
            ctr.v += n
            em(eng, lambda e, fn=fn, s=ctr.sem, n=n: fn(e).then_inc(s, n))
            return ctr.v

        def fence_dve():
            op("vector", lambda e: e.tensor_copy(out=junk[:], in_=junk[:]), eDVE)

        def fence_act():
            op("scalar", lambda e: e.activation(out=junk[:], in_=junk[:],
                                                func=AF.Identity), eACT)

        bank_last = {i: None for i in range(8)}

        def bank_acquire(eng, i):
            if bank_last[i] is not None:
                wait(eng, bank_last[i])

        def bank_release(i, ctr, val):
            bank_last[i] = (ctr.sem, val)

        dbg_tensors = {}

        def dbg_dump(name, src_ap, shape, dt, dep_ctr, dep_val,
                     stall=("vector", "scalar", "tensor")):
            if not dbg:
                return
            dtn = nc.dram_tensor("dbg_" + name, shape, dt, kind="ExternalOutput").ap()
            dbg_tensors[name] = dtn
            wait("gpsimd", dep_ctr, dep_val)
            v = op("gpsimd", lambda e, dtn=dtn, src_ap=src_ap: e.dma_start(
                out=dtn, in_=src_ap), cDBG, 16)
            for en in stall:
                wait(en, cDBG, v)

        for fn in [
            lambda e: e.dma_start(out=ident_f[:], in_=d_ident),
            lambda e: e.dma_start(out=ones_col_f[:], in_=d_ones_col),
            lambda e: e.dma_start(out=ones_row_f[:], in_=d_ones_row),
            lambda e: e.dma_start(out=ln_g[:], in_=d_ln_g),
            lambda e: e.dma_start(out=ln_b[:], in_=d_ln_b),
            lambda e: e.dma_start(out=conv_w[:], in_=d_convw),
            lambda e: e.dma_start(out=conv_b[:], in_=d_convb),
            lambda e: e.dma_start(out=dt_b[:], in_=d_dtb),
            lambda e: e.dma_start(out=A_neg[:], in_=d_logA),
            lambda e: e.dma_start(out=D_fm[:], in_=d_Dp),
            lambda e: e.dma_start(out=b1_fm[:], in_=d_b1),
            lambda e: e.dma_start(out=bg_row_s[:], in_=d_bg),
        ]:
            op("sync", fn, cSM, 16)
        op("sync", lambda e: e.dma_start(
            out=Wg_sb[:].rearrange("p (ks e) -> p ks e", e=NE),
            in_=d_Wg.rearrange("(ks p) e -> p ks e", p=128),
            allow_slow_non_contiguous=True), cSM, 16)
        v_small = cSM.v

        op("gpsimd", lambda e: e.dma_start(out=sel16_b[:], in_=d_sel16), cCVsm, 16)
        op("gpsimd", lambda e: e.dma_start(out=sel8_b[:], in_=d_sel8), cCVsm, 16)
        op("gpsimd", lambda e: e.dma_start(out=b2_b[:], in_=d_b2), cCVsm, 16)
        op("gpsimd", lambda e: e.dma_start(
            out=Bw_b[:].rearrange("p (ks s) -> p ks s", s=NS),
            in_=d_Bw.rearrange("(ks p) s -> p ks s", p=128),
            allow_slow_non_contiguous=True), cCVsm, 16)
        op("gpsimd", lambda e: e.dma_start(
            out=Cw_b[:].rearrange("p (ks s) -> p ks s", s=NS),
            in_=d_Cw.rearrange("(ks p) s -> p ks s", p=128),
            allow_slow_non_contiguous=True), cCVsm, 16)
        v_cast_small = cCVsm.v

        wait("scalar", cSM, v_small)
        v_exp = op("scalar", lambda e: e.activation(out=A_neg[:], in_=A_neg[:],
                                                    func=AF.Exp), eACT)
        wait("vector", eACT, v_exp)
        op("vector", lambda e: e.tensor_copy(out=ones_col_b[:], in_=ones_col_f[:]), eDVE)
        op("vector", lambda e: e.memset(h_state[:], 0.0), eDVE)
        op("vector", lambda e: e.memset(hist[:], 0.0), eDVE)
        op("vector", lambda e: e.memset(junk[:], 0.0), eDVE)
        op("vector", lambda e: e.tensor_scalar(out=A_neg[:], in0=A_neg[:],
                                               scalar1=-1.0, scalar2=None,
                                               op0=ALU.mult), eDVE)
        v_dset = eDVE.v
        wait("tensor", cSM, v_small)
        bank_acquire("tensor", 4)
        v_bgmm = op("tensor", lambda e: e.matmul(pb[4][:, 0:NE], ones_row_f[0:1, :],
                                                 bg_row_s[0:1, :],
                                                 start=True, stop=True), ePE, 1)
        wait("vector", ePE, v_bgmm)
        fence_dve()
        op("vector", lambda e: e.tensor_copy(out=bg_bcast[:], in_=pb[4][:, 0:NE]), eDVE)
        bank_release(4, eDVE, eDVE.v)

        for mt in range(16):
            src = d_Win.rearrange("(ks p) (mt m) -> mt p ks m", p=128, m=128)[mt]
            op("gpsimd", lambda e, src=src, mt=mt: e.dma_start(
                out=s_Win[mt].rearrange("p (ks m) -> p ks m", m=128), in_=src,
                allow_slow_non_contiguous=True), cCVF, 16)
        for mt in range(8):
            src = d_dtw.rearrange("(ks p) (mt m) -> mt p ks m", p=128, m=128)[mt]
            op("gpsimd", lambda e, src=src, mt=mt: e.dma_start(
                out=s_dtw[mt].rearrange("p (ks m) -> p ks m", m=128), in_=src,
                allow_slow_non_contiguous=True), cCVF, 16)
        for mt in range(8):
            src = d_Wout.rearrange("(ks p) (mt m) -> mt p ks m", p=128, m=128)[mt]
            op("gpsimd", lambda e, src=src, mt=mt: e.dma_start(
                out=s_Wout[mt].rearrange("p (ks m) -> p ks m", m=128), in_=src,
                allow_slow_non_contiguous=True), cCVF, 16)
        v_cv_front = cCVF.v
        v_cv_e = []
        for ee in range(NE):
            for hu in range(16):
                src = d_W1[ee].rearrange("(ks p) (hu h) -> hu p ks h", p=128, h=256)[hu]
                op("gpsimd", lambda e, src=src, ee=ee, hu=hu: e.dma_start(
                    out=s_W1[ee, hu].rearrange("p (ks h) -> p ks h", h=256), in_=src,
                    allow_slow_non_contiguous=True), cCVe[ee], 16)
            for dt in range(8):
                for hq in range(4):
                    src = d_W2[ee].rearrange(
                        "(hq hs p) (dt d) -> dt hq p hs d", hq=4, hs=8, p=128,
                        d=128)[dt, hq]
                    dst = s_W2[ee, dt].rearrange("p (hq hs d) -> hq p hs d",
                                                 hq=4, hs=8)[hq]
                    op("gpsimd", lambda e, src=src, dst=dst, ee=ee: e.dma_start(
                        out=dst, in_=src, allow_slow_non_contiguous=True), cCVe[ee], 16)
            v_cv_e.append((cCVe[ee].sem, cCVe[ee].v))

        xload_v = {}
        xload_consumed = {}

        def load_x_tile(gt):
            if gt - 2 in xload_consumed:
                wait("sync", ePE, xload_consumed[gt - 2])
            sl = (gt % 2) * 1024
            c_ = cXs[gt % 2]
            v = op("sync", lambda e, gt=gt, sl=sl: e.dma_start(
                out=x_tm[:, sl:sl + 1024], in_=dx[gt * 128:(gt + 1) * 128, :]), c_, 16)
            xload_v[gt] = (c_.sem, v)

        fw_state = {"count": 0, "loaded": {}, "consumed": {}}

        def stream_front_unit(scr_ap):
            u = fw_state["count"]
            if u >= 3:
                wait("sync", ePE, fw_state["consumed"][u - 3])
            sl = (u % 3) * 1024
            c_ = cFWs[u % 3]
            v = op("sync", lambda e, scr=scr_ap, sl=sl: e.dma_start(
                out=fw[:, sl:sl + 1024], in_=scr), c_, 16)
            fw_state["loaded"][u] = (c_.sem, v)
            fw_state["count"] = u + 1
            return u, sl

        chunk_S = {}
        pass_S = {}
        scan_state = {"v_pool_last": None}

        def fm_ln_stats_apply(src, ncols, writer, v_src_dve):
            op("vector", lambda e: e.tensor_copy(out=xsq_b[:, 0:NJ * ncols],
                                                 in_=src[:, 0:NJ * ncols]), eDVE)
            v_c1 = eDVE.v
            wait("scalar", eDVE, v_src_dve)
            op("scalar", lambda e: e.activation(out=z_silu[:, 0:NJ * ncols],
                                                in_=src[:, 0:NJ * ncols],
                                                func=AF.Square), eACT)
            v_c2 = eACT.v
            wait("tensor", eDVE, v_c1)
            wait("tensor", eACT, v_c2)
            bank_acquire("tensor", 4)
            bank_acquire("tensor", 5)
            for j in range(NJ):
                op("tensor", lambda e, j=j: e.matmul(
                    pb[4][0:1, 0:ncols], ones_col_b[:],
                    xsq_b[:, j * ncols:(j + 1) * ncols],
                    start=(j == 0), stop=(j == NJ - 1)), ePE, 1 if j == NJ - 1 else None)
            for j in range(NJ):
                op("tensor", lambda e, j=j: e.matmul(
                    pb[5][0:1, 0:ncols], ones_col_b[:],
                    z_silu[:, j * ncols:(j + 1) * ncols],
                    start=(j == 0), stop=(j == NJ - 1)), ePE, 1 if j == NJ - 1 else None)
            v_stat = ePE.v
            wait("vector", ePE, v_stat)
            fence_dve()
            op("vector", lambda e: e.tensor_scalar(out=mu_row[0:1, 0:ncols],
                                                   in0=pb[4][0:1, 0:ncols],
                                                   scalar1=1.0 / D, scalar2=None,
                                                   op0=ALU.mult), eDVE)
            op("vector", lambda e: e.tensor_mul(out=tmp_row[0:1, 0:ncols],
                                                in0=mu_row[0:1, 0:ncols],
                                                in1=mu_row[0:1, 0:ncols]), eDVE)
            op("vector", lambda e: e.scalar_tensor_tensor(
                out=var_row[0:1, 0:ncols], in0=pb[5][0:1, 0:ncols], scalar=1.0 / D,
                in1=tmp_row[0:1, 0:ncols], op0=ALU.mult, op1=ALU.subtract), eDVE)
            op("vector", lambda e: e.tensor_scalar(
                out=var_row[0:1, 0:ncols], in0=var_row[0:1, 0:ncols], scalar1=LN_EPS,
                scalar2=None, op0=ALU.add), eDVE)
            fence_dve()
            v_var = eDVE.v
            wait("scalar", eDVE, v_var)
            op("scalar", lambda e: e.activation(out=tmp_row[0:1, 0:ncols],
                                                in_=var_row[0:1, 0:ncols],
                                                func=AF.Ln), eACT)
            op("scalar", lambda e: e.activation(out=rstd_row[0:1, 0:ncols],
                                                in_=tmp_row[0:1, 0:ncols],
                                                func=AF.Exp, scale=-0.5), eACT)
            fence_act()
            v_rstd = eACT.v
            wait("tensor", eACT, v_rstd)
            op("tensor", lambda e: e.matmul(pb[4][:, 0:ncols], ones_row_f[0:1, :],
                                            mu_row[0:1, 0:ncols],
                                            start=True, stop=True), ePE)
            op("tensor", lambda e: e.matmul(pb[5][:, 0:ncols], ones_row_f[0:1, :],
                                            rstd_row[0:1, 0:ncols],
                                            start=True, stop=True), ePE, 1)
            v_bcst = ePE.v
            wait("vector", ePE, v_bcst)
            for j in range(NJ):
                writer(j)
            bank_release(4, eDVE, eDVE.v)
            bank_release(5, eDVE, eDVE.v)

        def make_front_stages(c):
            cs = chunk_S.setdefault(c, {})
            half = c % 2
            p_owner = c // 2
            stages = []

            def s_load():
                for tt in range(2):
                    load_x_tile(2 * c + tt)
            stages.append(s_load)

            def s_transpose():
                for tt in range(2):
                    gt = 2 * c + tt
                    sl = (gt % 2) * 1024
                    wait("tensor", xload_v[gt])
                    if c == 0 and tt == 0:
                        wait("tensor", cSM, v_small)
                    bank_acquire("tensor", 4)
                    bank_acquire("tensor", 5)
                    for j in range(NJ):
                        bank = pb[4] if j < 4 else pb[5]
                        colo = (j % 4) * 128
                        op("tensor", lambda e, sl=sl, j=j, bank=bank, colo=colo:
                           e.transpose(bank[:, colo:colo + 128],
                                       x_tm[:, sl + j * 128: sl + (j + 1) * 128],
                                       ident_f[:]), ePE, 1 if j == NJ - 1 else None)
                    v_tr = ePE.v
                    xload_consumed[gt] = v_tr
                    wait("vector", ePE, v_tr)
                    for bi, bank in ((0, pb[4]), (1, pb[5])):
                        dst = x_fm[:].rearrange("p (j t) -> p j t", j=NJ)[
                            :, bi * 4:(bi + 1) * 4, tt * 128:(tt + 1) * 128]
                        srcv = bank[:].rearrange("p (j t) -> p j t", j=4)
                        op("vector", lambda e, dst=dst, srcv=srcv: e.tensor_copy(
                            out=dst, in_=srcv), eDVE, 1)
                    bank_release(4, eDVE, eDVE.v)
                    bank_release(5, eDVE, eDVE.v)
                cs["v_xfm"] = eDVE.v
            stages.append(s_transpose)

            def s_ln1():
                if c == 0:
                    wait("vector", eDVE, v_dset)

                def writer(j):
                    op("vector", lambda e, j=j: e.tensor_sub(
                        out=hc[:, 0:LC], in0=x_fm[:, j * LC:(j + 1) * LC],
                        in1=pb[4][:, 0:LC]), eDVE)
                    op("vector", lambda e: e.tensor_mul(
                        out=hc[:, LC:2 * LC], in0=hc[:, 0:LC], in1=pb[5][:, 0:LC]),
                        eDVE)
                    op("vector", lambda e, j=j: e.tensor_scalar(
                        out=xn_b[:, j * LC:(j + 1) * LC], in0=hc[:, LC:2 * LC],
                        scalar1=ln_g[:, j:j + 1], scalar2=ln_b[:, j:j + 1],
                        op0=ALU.mult, op1=ALU.add), eDVE)
                fm_ln_stats_apply(x_fm, LC, writer, cs["v_xfm"])
                cs["v_xn"] = eDVE.v
            stages.append(s_ln1)

            def s_win(mt0, mt1):
                for mt in range(mt0, mt1):
                    if c == 0 and mt == 0:
                        wait("sync", cCVF, v_cv_front)
                    u, sl = stream_front_unit(s_Win[mt])
                    wait("tensor", fw_state["loaded"][u])
                    if mt == mt0:
                        wait("tensor", eDVE, cs["v_xn"])
                    bank_i = 4 + (mt % 4)
                    bank = pb[bank_i]
                    bank_acquire("tensor", bank_i)
                    for ks in range(8):
                        op("tensor", lambda e, sl=sl, ks=ks, bank=bank: e.matmul(
                            bank[:, 0:LC], fw[:, sl + ks * 128: sl + (ks + 1) * 128],
                            xn_b[:, ks * LC:(ks + 1) * LC],
                            start=(ks == 0), stop=(ks == 7)), ePE, 1 if ks == 7 else None)
                    fw_state["consumed"][u] = ePE.v
                    v_mm = ePE.v
                    if mt < 8:
                        j = mt
                        wait("vector", ePE, v_mm)
                        op("vector", lambda e, j=j: e.tensor_copy(
                            out=pre[:, 0:3], in_=hist[:, j * 3:(j + 1) * 3]), eDVE)
                        op("vector", lambda e, bank=bank: e.tensor_copy(
                            out=pre[:, 3:3 + LC], in_=bank[:, 0:LC]), eDVE)
                        bank_release(bank_i, eDVE, eDVE.v)
                        if cs.get("v_convsilu_prev") is not None:
                            wait("vector", eACT, cs["v_convsilu_prev"])
                        op("vector", lambda e, j=j: e.tensor_scalar(
                            out=sp1[:], in0=pre[:, 0:LC],
                            scalar1=conv_w[:, j * KCONV:j * KCONV + 1], scalar2=None,
                            op0=ALU.mult), eDVE)
                        for k in range(1, KCONV):
                            op("vector", lambda e, k=k, j=j: e.scalar_tensor_tensor(
                                out=sp1[:], in0=pre[:, k:k + LC],
                                scalar=conv_w[:, j * KCONV + k:j * KCONV + k + 1],
                                in1=sp1[:], op0=ALU.mult, op1=ALU.add), eDVE)
                        op("vector", lambda e, j=j: e.tensor_copy(
                            out=hist[:, j * 3:(j + 1) * 3], in_=pre[:, LC:LC + 3]),
                            eDVE)
                        v_conv = eDVE.v
                        wait("scalar", eDVE, v_conv)
                        op("scalar", lambda e, j=j: e.activation(
                            out=x_ssm[:, j * LC:(j + 1) * LC], in_=sp1[:],
                            func=AF.Silu, bias=conv_b[:, j:j + 1]), eACT)
                        cs["v_convsilu_prev"] = eACT.v
                    else:
                        j = mt - 8
                        wait("scalar", ePE, v_mm)
                        op("scalar", lambda e, j=j, bank=bank: e.activation(
                            out=z_silu[:, j * LC:(j + 1) * LC], in_=bank[:, 0:LC],
                            func=AF.Silu), eACT)
                        bank_release(bank_i, eACT, eACT.v)
                if mt1 == 16:
                    v_front_act = eACT.v
                    wait("vector", eACT, v_front_act)
                    op("vector", lambda e: e.tensor_copy(out=x_ssm_b[:], in_=x_ssm[:]),
                       eDVE)
                    cs["v_xssm_b"] = eDVE.v
            stages.append(lambda: s_win(0, 8))
            stages.append(lambda: s_win(8, 16))

            def s_dt(j0, j1):
                for j in range(j0, j1):
                    u, sl = stream_front_unit(s_dtw[j])
                    wait("tensor", fw_state["loaded"][u])
                    if j == j0:
                        wait("tensor", eDVE, cs["v_xssm_b"])
                    v_sp_war = cs.get("v_du_prev")
                    bank_i = 4 + (j % 4)
                    bank = pb[bank_i]
                    bank_acquire("tensor", bank_i)
                    for ks in range(8):
                        op("tensor", lambda e, sl=sl, ks=ks, bank=bank: e.matmul(
                            bank[:, 0:LC], fw[:, sl + ks * 128: sl + (ks + 1) * 128],
                            x_ssm_b[:, ks * LC:(ks + 1) * LC],
                            start=(ks == 0), stop=(ks == 7)), ePE, 1 if ks == 7 else None)
                    fw_state["consumed"][u] = ePE.v
                    v_mm = ePE.v
                    wait("scalar", ePE, v_mm)
                    if v_sp_war is not None:
                        wait("scalar", eDVE, v_sp_war)
                    op("scalar", lambda e, j=j, bank=bank: e.activation(
                        out=sp1[:], in_=bank[:, 0:LC], func=AF.Abs,
                        bias=dt_b[:, j:j + 1]), eACT)
                    op("scalar", lambda e: e.activation(out=sp1[:], in_=sp1[:],
                                                        func=AF.Exp, scale=-1.0), eACT)
                    op("scalar", lambda e: e.activation(out=sp1[:], in_=sp1[:],
                                                        func=AF.Ln, bias=1.0), eACT)
                    op("scalar", lambda e, j=j, bank=bank: e.activation(
                        out=sp2[:], in_=bank[:, 0:LC], func=AF.Relu,
                        bias=dt_b[:, j:j + 1]), eACT)
                    v_sp = eACT.v
                    bank_release(bank_i, eACT, v_sp)
                    wait("vector", eACT, v_sp)
                    op("vector", lambda e: e.tensor_add(out=dt_j[:], in0=sp1[:],
                                                        in1=sp2[:]), eDVE)
                    op("vector", lambda e, j=j: e.tensor_mul(
                        out=u_buf[:, j * LC:(j + 1) * LC], in0=dt_j[:],
                        in1=x_ssm[:, j * LC:(j + 1) * LC]), eDVE)
                    op("vector", lambda e, j=j: e.tensor_scalar(
                        out=sp2[:], in0=dt_j[:], scalar1=A_neg[:, j:j + 1],
                        scalar2=None, op0=ALU.mult), eDVE)
                    v_du = eDVE.v
                    cs["v_du_prev"] = v_du
                    wait("scalar", eDVE, v_du)
                    op("scalar", lambda e, j=j: e.activation(
                        out=decay[:, j * LC:(j + 1) * LC], in_=sp2[:], func=AF.Exp),
                        eACT)
                if j1 == NJ:
                    cs["v_decay"] = eACT.v
            stages.append(lambda: s_dt(0, 4))
            stages.append(lambda: s_dt(4, 8))

            def s_bc():
                wait("tensor", eDVE, cs["v_xssm_b"])
                if c == 0:
                    wait("tensor", cCVsm, v_cast_small)
                bank_acquire("tensor", 4)
                for w_i, wsb in ((0, Bw_b), (1, Cw_b)):
                    for tt in range(2):
                        for ks in range(8):
                            op("tensor", lambda e, w_i=w_i, wsb=wsb, tt=tt, ks=ks:
                               e.matmul(
                                   pb[4][:, (w_i * 2 + tt) * NS:
                                         (w_i * 2 + tt + 1) * NS],
                                   x_ssm_b[:, ks * LC + tt * 128:
                                           ks * LC + tt * 128 + 128],
                                   wsb[:, ks * NS:(ks + 1) * NS],
                                   start=(ks == 0), stop=(ks == 7)),
                               ePE, 1 if (w_i == 1 and tt == 1 and ks == 7) else None)
                v_bcmm = ePE.v
                wait("vector", ePE, v_bcmm)
                fence_dve()
                op("vector", lambda e: e.tensor_copy(out=Btm[:],
                                                     in_=pb[4][:, 0:2 * NS]), eDVE)
                op("vector", lambda e: e.tensor_copy(out=Ctm[:],
                                                     in_=pb[4][:, 2 * NS:4 * NS]), eDVE)
                bank_release(4, eDVE, eDVE.v)
                fence_dve()
                v_bctm = eDVE.v
                wait("tensor", eDVE, v_bctm)
                bank_acquire("tensor", 5)
                for w_i, wtm in ((0, Btm), (1, Ctm)):
                    for tt in range(2):
                        op("tensor", lambda e, w_i=w_i, wtm=wtm, tt=tt: e.transpose(
                            pb[5][0:NS, (w_i * 2 + tt) * 128:(w_i * 2 + tt + 1) * 128],
                            wtm[:, tt * NS:(tt + 1) * NS], ident_f[:]),
                            ePE, 1 if (w_i == 1 and tt == 1) else None)
                v_bcT = ePE.v
                wait("vector", ePE, v_bcT)
                fence_dve()
                op("vector", lambda e: e.tensor_copy(out=Bfm_b[:],
                                                     in_=pb[5][0:NS, 0:2 * 128]), eDVE)
                op("vector", lambda e: e.tensor_copy(out=Cfm_b[:],
                                                     in_=pb[5][0:NS, 2 * 128:4 * 128]),
                   eDVE)
                bank_release(5, eDVE, eDVE.v)
                fence_dve()
                cs["v_bcfm"] = eDVE.v
            stages.append(s_bc)

            def s_scan(g, jh):
                if jh == 0:
                    if g == 0:
                        wait("tensor", eDVE, cs["v_bcfm"])
                    for bank_i in (4, 5, 6, 7):
                        bank_acquire("tensor", bank_i)
                    for i in range(SG):
                        s_ = g * SG + i
                        bank = pb[6] if i < 2 else pb[7]
                        colo = (i % 2) * LC
                        op("tensor", lambda e, s_=s_, bank=bank, colo=colo: e.matmul(
                            bank[:, colo:colo + LC],
                            sel16_b[0:NS, s_ * 128:(s_ + 1) * 128],
                            Cfm_b[0:NS, :], start=True, stop=True),
                            ePE, 1 if i == SG - 1 else None)
                    for i in range(SG):
                        s_ = g * SG + i
                        bank = pb[4] if i < 2 else pb[5]
                        colo = (i % 2) * LC
                        op("tensor", lambda e, s_=s_, bank=bank, colo=colo: e.matmul(
                            bank[:, colo:colo + LC],
                            sel16_b[0:NS, s_ * 128:(s_ + 1) * 128],
                            Bfm_b[0:NS, :], start=True, stop=True),
                            ePE, 1 if i == SG - 1 else None)
                    cs["v_bg"] = ePE.v
                    wait("scalar", ePE, cs["v_bg"])
                    if scan_state["v_pool_last"] is not None:
                        wait("scalar", ePool, scan_state["v_pool_last"])
                    op("scalar", lambda e: e.activation(
                        out=B_sb[:, 0:2 * LC], in_=pb[4][:, 0:2 * LC],
                        func=AF.Identity), eACT)
                    op("scalar", lambda e: e.activation(
                        out=B_sb[:, 2 * LC:4 * LC], in_=pb[5][:, 0:2 * LC],
                        func=AF.Identity), eACT)
                    op("scalar", lambda e: e.activation(
                        out=C_sb[:, 0:2 * LC], in_=pb[6][:, 0:2 * LC],
                        func=AF.Identity), eACT)
                    op("scalar", lambda e: e.activation(
                        out=C_sb[:, 2 * LC:4 * LC], in_=pb[7][:, 0:2 * LC],
                        func=AF.Identity), eACT)
                    cs["v_bcsb"] = eACT.v
                    for bank_i in (4, 5, 6, 7):
                        bank_release(bank_i, eACT, cs["v_bcsb"])
                for j in range(jh * 4, jh * 4 + 4):
                    wait("gpsimd", eACT, cs["v_bcsb"])
                    if cs.get("v_hcfree") is not None:
                        wait("gpsimd", eDVE, cs["v_hcfree"])
                    for i in range(SG):
                        op("gpsimd", lambda e, j=j, i=i: e.tensor_mul(
                            out=hc[:, i * LC:(i + 1) * LC],
                            in0=u_buf[:, j * LC:(j + 1) * LC],
                            in1=B_sb[:, i * LC:(i + 1) * LC]),
                            ePool, 1 if i == SG - 1 else None)
                    v_d1 = ePool.v
                    wait("vector", ePool, v_d1)
                    for i in range(SG):
                        s_ = g * SG + i
                        op("vector", lambda e, j=j, i=i, s_=s_: e.tensor_tensor_scan(
                            out=h_traj[:, i * LC:(i + 1) * LC],
                            data0=decay[:, j * LC:(j + 1) * LC],
                            data1=hc[:, i * LC:(i + 1) * LC],
                            initial=h_state[:, j * NS + s_:j * NS + s_ + 1],
                            op0=ALU.mult, op1=ALU.add), eDVE,
                            1 if i == SG - 1 else None)
                    v_scans = eDVE.v
                    wait("gpsimd", eDVE, v_scans)
                    for i in range(SG):
                        op("gpsimd", lambda e, i=i: e.tensor_mul(
                            out=hc[:, i * LC:(i + 1) * LC],
                            in0=h_traj[:, i * LC:(i + 1) * LC],
                            in1=C_sb[:, i * LC:(i + 1) * LC]),
                            ePool, 1 if i == SG - 1 else None)
                    v_hcm = ePool.v
                    scan_state["v_pool_last"] = v_hcm
                    for i in range(SG):
                        op("vector", lambda e, j=j, g=g, i=i: e.tensor_copy(
                            out=h_state[:, j * NS + g * SG + i:
                                        j * NS + g * SG + i + 1],
                            in_=h_traj[:, i * LC + LC - 1:i * LC + LC]), eDVE)
                    wait("vector", ePool, v_hcm)
                    if g == 0:
                        op("vector", lambda e, j=j: e.tensor_reduce(
                            out=y_buf[:, j * LC:(j + 1) * LC],
                            in_=hc[:].rearrange("p (i t) -> p t i", i=SG),
                            axis=AX.X, op=ALU.add), eDVE)
                    else:
                        op("vector", lambda e: e.tensor_reduce(
                            out=d1s[:], in_=hc[:].rearrange("p (i t) -> p t i", i=SG),
                            axis=AX.X, op=ALU.add), eDVE)
                        op("vector", lambda e, j=j: e.tensor_add(
                            out=y_buf[:, j * LC:(j + 1) * LC],
                            in0=y_buf[:, j * LC:(j + 1) * LC], in1=d1s[:]), eDVE)
                    cs["v_hcfree"] = eDVE.v
                if dbg and c == 0 and jh == 1 and g == 0:
                    dbg_dump("yg0", y_buf[:], [128, NJ * LC], F32, eDVE, eDVE.v)
                    dbg_dump("htrj0", h_traj[:], [128, SG * LC], F32, eDVE, eDVE.v)
                    dbg_dump("hcg0", hc[:], [128, SG * LC], F32, eDVE, eDVE.v)
            for g in range(4):
                stages.append(lambda g=g: s_scan(g, 0))
                stages.append(lambda g=g: s_scan(g, 1))

            def s_y():
                if c == 0:
                    dbg_dump("yscan", y_buf[:], [128, NJ * LC], F32, eDVE, eDVE.v)
                    dbg_dump("ubuf", u_buf[:], [128, NJ * LC], BF16, eDVE, eDVE.v)
                    dbg_dump("Bsb", B_sb[:], [128, SG * LC], BF16, eACT, eACT.v)
                    dbg_dump("Csb", C_sb[:], [128, SG * LC], BF16, eACT, eACT.v)
                    dbg_dump("decayd", decay[:], [128, NJ * LC], F32, eDVE, eDVE.v)
                    dbg_dump("hst", h_state[:], [128, NJ * NS], F32, eDVE, eDVE.v)
                for j in range(NJ):
                    op("vector", lambda e, j=j: e.scalar_tensor_tensor(
                        out=y_buf[:, j * LC:(j + 1) * LC],
                        in0=x_ssm[:, j * LC:(j + 1) * LC],
                        scalar=D_fm[:, j:j + 1], in1=y_buf[:, j * LC:(j + 1) * LC],
                        op0=ALU.mult, op1=ALU.add), eDVE)
                    op("vector", lambda e, j=j: e.tensor_mul(
                        out=xn_b[:, j * LC:(j + 1) * LC],
                        in0=y_buf[:, j * LC:(j + 1) * LC],
                        in1=z_silu[:, j * LC:(j + 1) * LC]), eDVE)
                cs["v_yb"] = eDVE.v
            stages.append(s_y)

            def s_wout():
                for mt in range(8):
                    u, sl = stream_front_unit(s_Wout[mt])
                    wait("tensor", fw_state["loaded"][u])
                    if mt == 0:
                        wait("tensor", eDVE, cs["v_yb"])
                    bank_i = 4 + (mt % 4)
                    bank = pb[bank_i]
                    bank_acquire("tensor", bank_i)
                    for ks in range(8):
                        op("tensor", lambda e, sl=sl, ks=ks, bank=bank: e.matmul(
                            bank[:, 0:LC], fw[:, sl + ks * 128: sl + (ks + 1) * 128],
                            xn_b[:, ks * LC:(ks + 1) * LC],
                            start=(ks == 0), stop=(ks == 7)), ePE, 1 if ks == 7 else None)
                    fw_state["consumed"][u] = ePE.v
                    v_mm = ePE.v
                    wait("vector", ePE, v_mm)
                    op("vector", lambda e, mt=mt, bank=bank: e.scalar_tensor_tensor(
                        out=x_fm[:, mt * LC:(mt + 1) * LC],
                        in0=x_fm[:, mt * LC:(mt + 1) * LC], scalar=2.0,
                        in1=bank[:, 0:LC], op0=ALU.mult, op1=ALU.add), eDVE)
                    bank_release(bank_i, eDVE, eDVE.v)
                cs["v_s1"] = eDVE.v
            stages.append(s_wout)

            def s_ln2():
                if (p_owner - 1) in pass_S:
                    wait("vector", eDVE, pass_S[p_owner - 1]["v_oinit"])

                def writer(j):
                    xo = j * SC + half * LC
                    op("vector", lambda e, j=j: e.tensor_sub(
                        out=hc[:, 0:LC], in0=x_fm[:, j * LC:(j + 1) * LC],
                        in1=pb[4][:, 0:LC]), eDVE)
                    op("vector", lambda e, xo=xo: e.tensor_mul(
                        out=x2[:, xo:xo + LC], in0=hc[:, 0:LC], in1=pb[5][:, 0:LC]),
                        eDVE)
                fm_ln_stats_apply(x_fm, LC, writer, cs["v_s1"])
                cs["v_x2"] = eDVE.v
            stages.append(s_ln2)

            def s_x2b():
                if (p_owner - 1) in pass_S:
                    ps_prev = pass_S[p_owner - 1]
                    assert "v_ln3stat" in ps_prev, "x2b before LN3 stats"
                    wait("vector", ePE, ps_prev["v_ln3stat"])
                for j in range(NJ):
                    xo = j * SC + half * LC
                    op("vector", lambda e, xo=xo: e.tensor_copy(
                        out=x2b[:, xo:xo + LC], in_=x2[:, xo:xo + LC]), eDVE)
                cs["v_x2b"] = eDVE.v
            return stages, s_x2b

        w1s = {"count": 0, "loaded": {}, "consumed": {}}
        w2s = {"count": 0, "loaded": {}, "consumed": {}}
        out_dma_v = {}

        def make_moe_pass(p):
            ps = pass_S.setdefault(p, {})
            ps["v_oadd"] = {}
            ps["ounit"] = [0]
            tok0 = p * SC
            w2rot = [2, 3, 0, 1]

            def b_gate():
                wait("tensor", eDVE, chunk_S[2 * p + 1]["v_x2"])
                bank_acquire("tensor", 0)
                for tt in range(4):
                    for ks in range(8):
                        op("tensor", lambda e, tt=tt, ks=ks: e.matmul(
                            pb[0][:, tt * NE:(tt + 1) * NE],
                            x2[:, ks * SC + tt * 128: ks * SC + tt * 128 + 128],
                            Wg_sb[:, ks * NE:(ks + 1) * NE],
                            start=(ks == 0), stop=(ks == 7)),
                            ePE, 1 if (tt == 3 and ks == 7) else None)
                v_gmm = ePE.v
                wait("vector", ePE, v_gmm)
                fence_dve()
                for tt in range(4):
                    op("vector", lambda e, tt=tt: e.tensor_add(
                        out=gate_tm[:, tt * NE:(tt + 1) * NE],
                        in0=pb[0][:, tt * NE:(tt + 1) * NE], in1=bg_bcast[:]), eDVE)
                bank_release(0, eDVE, eDVE.v)
                for tt in range(4):
                    op("vector", lambda e, tt=tt: e.reduce_max(
                        out=gsc[:, tt:tt + 1], in_=gate_tm[:, tt * NE:(tt + 1) * NE],
                        axis=AX.X), eDVE)
                fence_dve()
                op("vector", lambda e: e.tensor_scalar(
                    out=gsc[:, 0:4], in0=gsc[:, 0:4], scalar1=-1.0, scalar2=None,
                    op0=ALU.mult), eDVE)
                fence_dve()
                v_m = eDVE.v
                wait("scalar", eDVE, v_m)
                for tt in range(4):
                    go = tt * NE
                    op("scalar", lambda e, go=go, tt=tt: e.activation(
                        out=probs[:, go:go + NE], in_=gate_tm[:, go:go + NE],
                        func=AF.Exp, bias=gsc[:, tt:tt + 1]), eACT)
                fence_act()
                v_e = eACT.v
                wait("vector", eACT, v_e)
                for tt in range(4):
                    op("vector", lambda e, tt=tt: e.reduce_sum(
                        out=gsc[:, 4 + tt:5 + tt],
                        in_=probs[:, tt * NE:(tt + 1) * NE], axis=AX.X), eDVE)
                fence_dve()
                op("vector", lambda e: e.reciprocal(out=gsc[:, 4:8], in_=gsc[:, 4:8]),
                   eDVE)
                fence_dve()
                for tt in range(4):
                    go = tt * NE
                    op("vector", lambda e, go=go, tt=tt: e.tensor_scalar(
                        out=probs[:, go:go + NE], in0=probs[:, go:go + NE],
                        scalar1=gsc[:, 4 + tt:5 + tt], scalar2=None, op0=ALU.mult),
                        eDVE)
                fence_dve()
                for tt in range(4):
                    op("vector", lambda e, tt=tt: e.reduce_max(
                        out=w_tm[:, tt:tt + 1], in_=probs[:, tt * NE:(tt + 1) * NE],
                        axis=AX.X), eDVE)
                fence_dve()
                for tt in range(4):
                    go = tt * NE
                    op("vector", lambda e, go=go, tt=tt: e.tensor_scalar(
                        out=mask_tm[:, go:go + NE], in0=probs[:, go:go + NE],
                        scalar1=w_tm[:, tt:tt + 1], scalar2=None, op0=ALU.is_ge), eDVE)
                fence_dve()
                for tt in range(4):
                    go = tt * NE
                    op("vector", lambda e, go=go, tt=tt: e.tensor_scalar(
                        out=mask_tm[:, go:go + NE], in0=mask_tm[:, go:go + NE],
                        scalar1=w_tm[:, tt:tt + 1], scalar2=None, op0=ALU.mult), eDVE)
                fence_dve()
                v_mask = eDVE.v
                if dbg and p == 0:
                    dbg_dump("probs", probs[:], [128, 4 * NE], F32, eDVE, v_mask)
                    dbg_dump("maskwtm", mask_tm[:], [128, 4 * NE], F32, eDVE, v_mask)
                wait("tensor", eDVE, v_mask)
                bank_acquire("tensor", 1)
                for tt in range(4):
                    op("tensor", lambda e, tt=tt: e.transpose(
                        pb[1][0:NE, tt * 128:(tt + 1) * 128],
                        mask_tm[:, tt * NE:(tt + 1) * NE], ident_f[:]),
                        ePE, 1 if tt == 3 else None)
                v_maskT = ePE.v
                wait("vector", ePE, v_maskT)
                fence_dve()
                op("vector", lambda e: e.tensor_copy(out=maskw_fm_b[:],
                                                     in_=pb[1][0:NE, 0:SC]), eDVE)
                bank_release(1, eDVE, eDVE.v)
                fence_dve()
                v_maskf = eDVE.v
                wait("tensor", eDVE, v_maskf)
                for dt in range(8):
                    bank_i = dt % 2
                    bank = pb[bank_i]
                    bank_acquire("tensor", bank_i)
                    op("tensor", lambda e, dt=dt, bank=bank: e.matmul(
                        bank[:, 0:SC], b2_b[0:NE, dt * 128:(dt + 1) * 128],
                        maskw_fm_b[0:NE, :], start=True, stop=True), ePE, 1)
                    v_b2 = ePE.v
                    wait("vector", ePE, v_b2)
                    op("vector", lambda e, dt=dt, bank=bank: e.tensor_add(
                        out=o_acc[:, dt * SC:(dt + 1) * SC],
                        in0=x2[:, dt * SC:(dt + 1) * SC], in1=bank[:, 0:SC]), eDVE, 1)
                    bank_release(bank_i, eDVE, eDVE.v)
                ps["v_oinit"] = eDVE.v

            def b_w1(ee, huh):
                if huh == 0:
                    bank_acquire("tensor", 2)
                    op("tensor", lambda e, ee=ee: e.matmul(
                        pb[2][:, 0:SC], sel8_b[0:NE, ee * 128:(ee + 1) * 128],
                        maskw_fm_b[0:NE, :], start=True, stop=True), ePE, 1)
                    v_mb = ePE.v
                    wait("vector", ePE, v_mb)
                    op("vector", lambda e: e.tensor_copy(out=mask_b[:],
                                                         in_=pb[2][:, 0:SC]), eDVE)
                    bank_release(2, eDVE, eDVE.v)
                    ps["v_mbs"] = eDVE.v
                for hu in range(huh * 8, huh * 8 + 8):
                    u = w1s["count"]
                    if u >= 3:
                        wait("sync", ePE, w1s["consumed"][u - 3])
                    sl = (u % 3) * 2048
                    if p == 0 and hu == 0 and huh == 0:
                        wait("sync", v_cv_e[ee])
                    c_ = cW1s[u % 3]
                    v = op("sync", lambda e, ee=ee, hu=hu, sl=sl: e.dma_start(
                        out=w1r[:, sl:sl + 2048], in_=s_W1[ee, hu]), c_, 16)
                    w1s["loaded"][u] = (c_.sem, v)
                    w1s["count"] = u + 1
                    wait("tensor", w1s["loaded"][u])
                    if hu == 0 and huh == 0:
                        wait("tensor", eDVE, chunk_S[2 * p + 1]["v_x2b"])
                    for ht in range(2):
                        hidx = hu * 2 + ht
                        bank_i = hidx % 4
                        bank = pb[bank_i]
                        bank_acquire("tensor", bank_i)
                        for ks in range(8):
                            op("tensor", lambda e, sl=sl, ks=ks, ht=ht, bank=bank:
                               e.matmul(
                                   bank[:, 0:SC],
                                   w1r[:, sl + ks * 256 + ht * 128:
                                       sl + ks * 256 + ht * 128 + 128],
                                   x2b[:, ks * SC:(ks + 1) * SC],
                                   start=(ks == 0), stop=(ks == 7)),
                               ePE, 1 if ks == 7 else None)
                        v_mm = ePE.v
                        if ht == 1:
                            w1s["consumed"][u] = v_mm
                        if ee == NE - 1 and hidx == 31:
                            ps["v_w1_last"] = v_mm
                        wait("scalar", ePE, v_mm)
                        v_silu = op(
                            "scalar", lambda e, ee=ee, hidx=hidx, bank=bank:
                            e.activation(
                                out=h1[:, hidx * SC:(hidx + 1) * SC],
                                in_=bank[:, 0:SC], func=AF.Silu,
                                bias=b1_fm[:, ee * 32 + hidx:ee * 32 + hidx + 1]),
                            eACT, 1)
                        bank_release(bank_i, eACT, v_silu)
                if huh == 1:
                    ps["v_h1"] = eACT.v

            def b_w2(ee, dth):
                for dt in range(dth * 4, dth * 4 + 4):
                    if p == 0 and dt == 0 and dth == 0:
                        wait("sync", v_cv_e[ee])
                    gi = ps["ounit"][0]
                    bank_i = [2, 3, 0, 1][gi % 4]
                    bank = pb[bank_i]
                    bank_acquire("tensor", bank_i)
                    for hh in range(2):
                        u = w2s["count"]
                        if u >= 2:
                            wait("sync", ePE, w2s["consumed"][u - 2])
                        sl = (u % 2) * 2048
                        c_ = cW2s[u % 2]
                        v = op("sync", lambda e, ee=ee, dt=dt, hh=hh, sl=sl:
                               e.dma_start(
                                   out=w2r[:, sl:sl + 2048],
                                   in_=s_W2[ee, dt][:, hh * 2048:(hh + 1) * 2048]),
                               c_, 16)
                        w2s["loaded"][u] = (c_.sem, v)
                        w2s["count"] = u + 1
                        wait("tensor", w2s["loaded"][u])
                        if dt == 0 and dth == 0 and hh == 0:
                            wait("tensor", eACT, ps["v_h1"])
                        for hs in range(16):
                            hsg = hh * 16 + hs
                            op("tensor", lambda e, sl=sl, hs=hs, hsg=hsg, bank=bank:
                               e.matmul(
                                   bank[:, 0:SC],
                                   w2r[:, sl + hs * 128:sl + (hs + 1) * 128],
                                   h1[:, hsg * SC:(hsg + 1) * SC],
                                   start=(hsg == 0), stop=(hsg == 31)),
                               ePE, 1 if hs == 15 else None)
                        w2s["consumed"][u] = ePE.v
                    v_mm = ePE.v
                    wait("vector", ePE, v_mm)
                    if ee == 0 and dt == 0:
                        wait("vector", eDVE, ps["v_mbs"])
                    op("vector", lambda e, bank=bank: e.tensor_mul(
                        out=tmp_o[:], in0=bank[:, 0:SC], in1=mask_b[:]), eDVE)
                    vv = op("vector", lambda e, dt=dt: e.tensor_add(
                        out=o_acc[:, dt * SC:(dt + 1) * SC],
                        in0=o_acc[:, dt * SC:(dt + 1) * SC], in1=tmp_o[:]), eDVE, 1)
                    ps["v_oadd"][gi] = vv
                    bank_release(bank_i, eDVE, vv)
                    ps["ounit"][0] += 1

            def b_ln3out():
                v_s2 = ps["v_oadd"][ps["ounit"][0] - 1]
                if dbg and p == 0:
                    dbg_dump("oacc", o_acc[:], [128, NJ * SC], F32, eDVE, v_s2)
                op("vector", lambda e: e.tensor_copy(out=x2b[:], in_=o_acc[:]), eDVE)
                v_c1 = eDVE.v
                wait("scalar", eDVE, v_s2)
                op("scalar", lambda e: e.activation(out=h1[:, 0:NJ * SC], in_=o_acc[:],
                                                    func=AF.Square), eACT)
                v_c2 = eACT.v
                wait("tensor", eDVE, v_c1)
                wait("tensor", eACT, v_c2)
                bank_acquire("tensor", 0)
                bank_acquire("tensor", 1)
                for j in range(NJ):
                    op("tensor", lambda e, j=j: e.matmul(
                        pb[0][0:1, 0:SC], ones_col_b[:], x2b[:, j * SC:(j + 1) * SC],
                        start=(j == 0), stop=(j == NJ - 1)),
                        ePE, 1 if j == NJ - 1 else None)
                for j in range(NJ):
                    op("tensor", lambda e, j=j: e.matmul(
                        pb[1][0:1, 0:SC], ones_col_b[:], h1[:, j * SC:(j + 1) * SC],
                        start=(j == 0), stop=(j == NJ - 1)),
                        ePE, 1 if j == NJ - 1 else None)
                v_stat = ePE.v
                ps["v_ln3stat"] = v_stat
                wait("vector", ePE, v_stat)
                fence_dve()
                op("vector", lambda e: e.tensor_scalar(out=mu_row[0:1, 0:SC],
                                                       in0=pb[0][0:1, 0:SC],
                                                       scalar1=1.0 / D, scalar2=None,
                                                       op0=ALU.mult), eDVE)
                op("vector", lambda e: e.tensor_mul(out=tmp_row[0:1, 0:SC],
                                                    in0=mu_row[0:1, 0:SC],
                                                    in1=mu_row[0:1, 0:SC]), eDVE)
                op("vector", lambda e: e.scalar_tensor_tensor(
                    out=var_row[0:1, 0:SC], in0=pb[1][0:1, 0:SC], scalar=1.0 / D,
                    in1=tmp_row[0:1, 0:SC], op0=ALU.mult, op1=ALU.subtract), eDVE)
                op("vector", lambda e: e.tensor_scalar(
                    out=var_row[0:1, 0:SC], in0=var_row[0:1, 0:SC], scalar1=LN_EPS,
                    scalar2=None, op0=ALU.add), eDVE)
                fence_dve()
                v_var = eDVE.v
                wait("scalar", eDVE, v_var)
                op("scalar", lambda e: e.activation(out=tmp_row[0:1, 0:SC],
                                                    in_=var_row[0:1, 0:SC],
                                                    func=AF.Ln), eACT)
                op("scalar", lambda e: e.activation(out=rstd_row[0:1, 0:SC],
                                                    in_=tmp_row[0:1, 0:SC],
                                                    func=AF.Exp, scale=-0.5), eACT)
                fence_act()
                v_rstd = eACT.v
                wait("tensor", eACT, v_rstd)
                op("tensor", lambda e: e.matmul(pb[0][:, 0:SC], ones_row_f[0:1, :],
                                                mu_row[0:1, 0:SC],
                                                start=True, stop=True), ePE)
                op("tensor", lambda e: e.matmul(pb[1][:, 0:SC], ones_row_f[0:1, :],
                                                rstd_row[0:1, 0:SC],
                                                start=True, stop=True), ePE, 1)
                v_bcst = ePE.v
                wait("vector", ePE, v_bcst)
                for j in range(NJ):
                    op("vector", lambda e, j=j: e.tensor_sub(
                        out=tmp_o[:], in0=o_acc[:, j * SC:(j + 1) * SC],
                        in1=pb[0][:, 0:SC]), eDVE)
                    op("vector", lambda e, j=j: e.tensor_mul(
                        out=o_acc[:, j * SC:(j + 1) * SC], in0=tmp_o[:],
                        in1=pb[1][:, 0:SC]), eDVE)
                bank_release(0, eDVE, eDVE.v)
                bank_release(1, eDVE, eDVE.v)
                v_x3 = eDVE.v
                wait("tensor", eDVE, v_x3)
                for tt in range(4):
                    bank_acquire("tensor", 0)
                    bank_acquire("tensor", 1)
                    for j in range(NJ):
                        bank = pb[0] if j < 4 else pb[1]
                        colo = (j % 4) * 128
                        op("tensor", lambda e, j=j, tt=tt, bank=bank, colo=colo:
                           e.transpose(
                               bank[:, colo:colo + 128],
                               o_acc[:, j * SC + tt * 128: j * SC + tt * 128 + 128],
                               ident_f[:]), ePE, 1 if j == NJ - 1 else None)
                    v_trT = ePE.v
                    wait("vector", ePE, v_trT)
                    prev = (p, tt - 1) if tt > 0 else (p - 1, 3)
                    if prev in out_dma_v:
                        wait("vector", cOUT, out_dma_v[prev])
                    for bi, bank in ((0, pb[0]), (1, pb[1])):
                        op("vector", lambda e, bi=bi, bank=bank: e.tensor_copy(
                            out=out_tm[:, bi * 512:(bi + 1) * 512],
                            in_=bank[:, 0:512]), eDVE, 1)
                    v_evT = eDVE.v
                    bank_release(0, eDVE, v_evT)
                    bank_release(1, eDVE, v_evT)
                    wait("gpsimd", eDVE, v_evT)
                    out_dma_v[(p, tt)] = op("gpsimd", lambda e, tt=tt, tok0=tok0:
                                            e.dma_start(
                                                out=d_out[tok0 + tt * 128:
                                                          tok0 + (tt + 1) * 128, :],
                                                in_=out_tm[:]), cOUT, 16)

            expert_blocks = []
            for ee in range(NE):
                expert_blocks.append(lambda ee=ee: b_w1(ee, 0))
                expert_blocks.append(lambda ee=ee: b_w1(ee, 1))
                expert_blocks.append(lambda ee=ee: b_w2(ee, 0))
                expert_blocks.append(lambda ee=ee: b_w2(ee, 1))
            return b_gate, expert_blocks, b_ln3out

        for c in (0, 1):
            if c < nchunk:
                stages, x2b_stage = make_front_stages(c)
                for stg in stages:
                    stg()
                x2b_stage()
        for p in range(npass):
            b_gate, expert_blocks, b_ln3out = make_moe_pass(p)
            fstages = []
            x2b_stages = []
            for cc in (2 * p + 2, 2 * p + 3):
                if cc < nchunk:
                    stages, x2b_stage = make_front_stages(cc)
                    fstages += stages
                    x2b_stages.append(x2b_stage)
            b_gate()
            nb = len(expert_blocks)
            nf = len(fstages)
            fi = 0
            for bi, blk_fn in enumerate(expert_blocks):
                blk_fn()
                want = ((bi + 1) * nf) // nb
                while fi < want:
                    fstages[fi]()
                    fi += 1
            while fi < nf:
                fstages[fi]()
                fi += 1
            b_ln3out()
            for stg in x2b_stages:
                stg()

        wait("gpsimd", cOUT, cOUT.v)
        if dbg:
            wait("gpsimd", cDBG, cDBG.v)

        counts = {k: len(v) for k, v in prog.items()}

        def runner(name):
            def body(eng):
                for fn in prog[name]:
                    fn(eng)
            return body

        blk = st.enter_context(nc.Block())
        blk.sync(runner("sync"))
        blk.tensor(runner("tensor"))
        blk.vector(runner("vector"))
        blk.scalar(runner("scalar"))
        blk.gpsimd(runner("gpsimd"))

    nc._instr_counts = counts
    return nc


def host_inputs(x_core, w):
    def fm(v):
        return np.ascontiguousarray(np.asarray(v, np.float32).reshape(NJ, 128).T)

    sel16 = np.zeros((NS, NS * 128), np.float32)
    for s in range(NS):
        sel16[s, s * 128:(s + 1) * 128] = 1.0
    sel8 = np.zeros((NE, NE * 128), np.float32)
    for s in range(NE):
        sel8[s, s * 128:(s + 1) * 128] = 1.0
    b1 = np.asarray(w["L0_ffn_b1"], np.float32)
    b1_fm = np.ascontiguousarray(
        b1.reshape(NE, 32, 128).transpose(2, 0, 1).reshape(128, NE * 32))
    conv_w = np.asarray(w["L0_attn_conv_w"], np.float32)[:, 0, :]
    conv_w_fm = np.ascontiguousarray(
        conv_w.reshape(NJ, 128, KCONV).transpose(1, 0, 2).reshape(128, NJ * KCONV))
    f32 = lambda k: np.ascontiguousarray(np.asarray(w[k], np.float32))
    return dict(
        x=np.ascontiguousarray(np.asarray(x_core, np.float32)),
        ln_g_fm=fm(w["L0_attn_ln_g"]), ln_b_fm=fm(w["L0_attn_ln_b"]),
        W_in=f32("L0_attn_W_in"), conv_w_fm=conv_w_fm,
        conv_b_fm=fm(w["L0_attn_conv_b"]),
        dt_w=f32("L0_attn_dt_w"), dt_b_fm=fm(w["L0_attn_dt_b"]),
        B_w=f32("L0_attn_B_w"), C_w=f32("L0_attn_C_w"),
        log_A_fm=fm(w["L0_attn_log_A"]), D_fm=fm(w["L0_attn_D_param"]),
        W_out=f32("L0_attn_W_out"), Wg=f32("L0_ffn_Wg"),
        bg_row=np.ascontiguousarray(np.asarray(w["L0_ffn_bg"], np.float32)
                                    .reshape(1, NE)),
        W1=f32("L0_ffn_W1"), b1_fm=b1_fm, W2=f32("L0_ffn_W2"), b2=f32("L0_ffn_b2"),
        ident=np.eye(128, dtype=np.float32),
        ones_col=np.ones((128, 1), np.float32),
        ones_row=np.ones((1, 128), np.float32),
        sel16=sel16, sel8=sel8,
    )


def kernel(**inputs):
    x = np.asarray(inputs["L0_attn_x"])
    B, T, _ = x.shape
    nc = build_nc(T, detect_races=False)
    in_maps = [host_inputs(x[i], inputs) for i in range(B)]
    res = run_bass_kernel_spmd(nc, in_maps, core_ids=list(range(B)))
    return np.stack([r["out"] for r in res.results], axis=0)


# version 12
# speedup vs baseline: 1.1281x; 1.1281x over previous
import contextlib

import numpy as np

import concourse.bass as bass
import concourse.mybir as mybir
from concourse.bass_utils import run_bass_kernel_spmd

F32 = mybir.dt.float32
BF16 = mybir.dt.bfloat16
AF = mybir.ActivationFunctionType
ALU = mybir.AluOpType
AX = mybir.AxisListType

D = 1024
NS = 16
KCONV = 4
DH = 4096
NE = 8
NJ = 8
LC = 256
SC = 512
SG = 4
LN_EPS = 1e-5

ENGINES = ("sync", "tensor", "vector", "scalar", "gpsimd")


class Ctr:
    def __init__(self, st, nc, name):
        self.sem = st.enter_context(nc.semaphore(name))
        self.v = 0


def build_nc(T, detect_races=True, dbg=False):
    nchunk = T // LC
    npass = T // SC
    nc = bass.Bass("TRN2", target_bir_lowering=False,
                   detect_race_conditions=detect_races)

    dx = nc.dram_tensor("x", [T, D], F32, kind="ExternalInput").ap()
    d_ln_g = nc.dram_tensor("ln_g_fm", [128, NJ], F32, kind="ExternalInput").ap()
    d_ln_b = nc.dram_tensor("ln_b_fm", [128, NJ], F32, kind="ExternalInput").ap()
    d_Win = nc.dram_tensor("W_in", [D, 2 * D], F32, kind="ExternalInput").ap()
    d_convw = nc.dram_tensor("conv_w_fm", [128, NJ * KCONV], F32, kind="ExternalInput").ap()
    d_convb = nc.dram_tensor("conv_b_fm", [128, NJ], F32, kind="ExternalInput").ap()
    d_dtw = nc.dram_tensor("dt_w", [D, D], F32, kind="ExternalInput").ap()
    d_dtb = nc.dram_tensor("dt_b_fm", [128, NJ], F32, kind="ExternalInput").ap()
    d_Bw = nc.dram_tensor("B_w", [D, NS], F32, kind="ExternalInput").ap()
    d_Cw = nc.dram_tensor("C_w", [D, NS], F32, kind="ExternalInput").ap()
    d_logA = nc.dram_tensor("log_A_fm", [128, NJ], F32, kind="ExternalInput").ap()
    d_Dp = nc.dram_tensor("D_fm", [128, NJ], F32, kind="ExternalInput").ap()
    d_Wout = nc.dram_tensor("W_out", [D, D], F32, kind="ExternalInput").ap()
    d_Wg = nc.dram_tensor("Wg", [D, NE], F32, kind="ExternalInput").ap()
    d_bg = nc.dram_tensor("bg_row", [1, NE], F32, kind="ExternalInput").ap()
    d_W1 = nc.dram_tensor("W1", [NE, D, DH], F32, kind="ExternalInput").ap()
    d_b1 = nc.dram_tensor("b1_fm", [128, NE * 32], F32, kind="ExternalInput").ap()
    d_W2 = nc.dram_tensor("W2", [NE, DH, D], F32, kind="ExternalInput").ap()
    d_b2 = nc.dram_tensor("b2", [NE, D], F32, kind="ExternalInput").ap()
    d_ident = nc.dram_tensor("ident", [128, 128], F32, kind="ExternalInput").ap()
    d_ones_col = nc.dram_tensor("ones_col", [128, 1], F32, kind="ExternalInput").ap()
    d_ones_row = nc.dram_tensor("ones_row", [1, 128], F32, kind="ExternalInput").ap()
    d_sel16 = nc.dram_tensor("sel16", [NS, NS * 128], F32, kind="ExternalInput").ap()
    d_sel8 = nc.dram_tensor("sel8", [NE, NE * 128], F32, kind="ExternalInput").ap()

    d_out = nc.dram_tensor("out", [T, D], F32, kind="ExternalOutput").ap()

    s_Win = nc.dram_tensor("s_Win", [16, 128, 8 * 128], BF16, kind="Internal").ap()
    s_dtw = nc.dram_tensor("s_dtw", [8, 128, 8 * 128], BF16, kind="Internal").ap()
    s_Wout = nc.dram_tensor("s_Wout", [8, 128, 8 * 128], BF16, kind="Internal").ap()
    s_W1 = nc.dram_tensor("s_W1", [NE, 16, 128, 8 * 256], BF16, kind="Internal").ap()
    s_W2 = nc.dram_tensor("s_W2", [NE, 8, 128, 32 * 128], BF16, kind="Internal").ap()

    st = contextlib.ExitStack()
    with st:
        def sb(name, shape, dt):
            return st.enter_context(nc.sbuf_tensor(name, shape, dt))

        ident_f = sb("ident_f", [128, 128], F32)
        ones_col_f = sb("ones_col_f", [128, 1], F32)
        ones_col_b = sb("ones_col_b", [128, 1], BF16)
        ones_row_f = sb("ones_row_f", [1, 128], F32)
        sel16_b = sb("sel16_b", [NS, NS * 128], BF16)
        sel8_b = sb("sel8_b", [NE, NE * 128], BF16)
        ln_g = sb("ln_g", [128, NJ], F32)
        ln_b = sb("ln_b", [128, NJ], F32)
        conv_w = sb("conv_w", [128, NJ * KCONV], F32)
        conv_b = sb("conv_b", [128, NJ], F32)
        dt_b = sb("dt_b", [128, NJ], F32)
        A_neg = sb("A_neg", [128, NJ], F32)
        D_fm = sb("D_fm_s", [128, NJ], F32)
        b1_fm = sb("b1_fm_s", [128, NE * 32], F32)
        b2_b = sb("b2_b", [NE, D], BF16)
        bg_row_s = sb("bg_row_s", [1, NE], F32)
        bg_bcast = sb("bg_bcast", [128, NE], F32)
        Wg_sb = sb("Wg_sb", [128, 8 * NE], F32)
        Bw_b = sb("Bw_b", [128, 8 * NS], BF16)
        Cw_b = sb("Cw_b", [128, 8 * NS], BF16)
        h_state = sb("h_state", [128, NJ * NS], BF16)
        hist = sb("hist", [128, NJ * 3], F32)
        pre = sb("pre", [128, LC + 3], F32)
        Btm = sb("Btm", [128, 2 * NS], F32)
        Ctm = sb("Ctm", [128, 2 * NS], F32)
        Bfm_b = sb("Bfm_b", [NS, LC], BF16)
        Cfm_b = sb("Cfm_b", [NS, LC], BF16)
        mu_row = sb("mu_row", [1, SC], F32)
        var_row = sb("var_row", [1, SC], F32)
        rstd_row = sb("rstd_row", [1, SC], F32)
        tmp_row = sb("tmp_row", [1, SC], F32)
        junk = sb("junk", [128, 256], F32)

        fw = sb("fw", [128, 3 * 1024], BF16)
        w1r = sb("w1r", [128, 3 * 2048], BF16)
        w2r = sb("w2r", [128, 2 * 2048], BF16)

        x_tm = sb("x_tm", [128, 2 * 1024], F32)
        out_tm = sb("out_tm", [128, 1024], F32)
        x_fm = sb("x_fm", [128, NJ * LC], F32)
        xn_b = sb("xn_b", [128, NJ * LC], BF16)
        xsq_b = sb("xsq_b", [128, NJ * LC], BF16)
        z_silu = sb("z_silu", [128, NJ * LC], BF16)
        x_ssm = sb("x_ssm", [128, NJ * LC], F32)
        x_ssm_b = sb("x_ssm_b", [128, NJ * LC], BF16)
        u_buf = sb("u_buf", [128, NJ * LC], BF16)
        B_sb = sb("B_sb", [128, SG * LC], BF16)
        C_sb = sb("C_sb", [128, SG * LC], BF16)
        hcb = sb("hcb", [128, SG * LC], BF16)
        d1b = sb("d1b", [128, LC], BF16)
        decay = sb("decay", [128, NJ * LC], F32)
        dt_j = sb("dt_j", [128, LC], F32)
        sp1 = sb("sp1", [128, LC], F32)
        sp2 = sb("sp2", [128, LC], F32)
        d1s = sb("d1s", [128, LC], F32)
        h_traj = sb("h_traj", [128, SG * LC], BF16)
        hc = sb("hc", [128, 2 * LC], F32)
        y_buf = sb("y_buf", [128, NJ * LC], F32)

        x2 = sb("x2", [128, NJ * SC], F32)
        x2b = sb("x2b", [128, NJ * SC], BF16)
        h1 = sb("h1", [128, 32 * SC], BF16)
        o_acc = sb("o_acc", [128, NJ * SC], F32)
        tmp_o = sb("tmp_o", [128, SC], F32)
        mask_b = sb("mask_b", [128, SC], F32)
        gate_tm = sb("gate_tm", [128, 4 * NE], F32)
        probs = sb("probs", [128, 4 * NE], F32)
        mask_tm = sb("mask_tm", [128, 4 * NE], F32)
        gsc = sb("gsc", [128, 8], F32)
        w_tm = sb("w_tm", [128, 4], F32)
        maskw_fm_b = sb("maskw_fm_b", [NE, SC], BF16)

        pb = [st.enter_context(nc.psum_tensor(f"pb{i}", [128, 512], F32))
              for i in range(8)]

        cSM = Ctr(st, nc, "cSM")
        cXs = [Ctr(st, nc, f"cX{i}") for i in range(2)]
        cFWs = [Ctr(st, nc, f"cFW{i}") for i in range(3)]
        cW1s = [Ctr(st, nc, f"cW1{i}") for i in range(3)]
        cW2s = [Ctr(st, nc, f"cW2{i}") for i in range(2)]
        cCVsm = Ctr(st, nc, "cCVsm")
        cCVF = Ctr(st, nc, "cCVF")
        cCVdt = Ctr(st, nc, "cCVdt")
        cCVwo = Ctr(st, nc, "cCVwo")
        cCVe = [Ctr(st, nc, f"cCVe{i}") for i in range(NE)]
        cOUT = Ctr(st, nc, "cOUT")
        cDBG = Ctr(st, nc, "cDBG")
        ePE = Ctr(st, nc, "ePE")
        eDVE = Ctr(st, nc, "eDVE")
        eACT = Ctr(st, nc, "eACT")

        prog = {e: [] for e in ENGINES}

        def em(eng, fn):
            prog[eng].append(fn)

        def wait(eng, ctr, val=None):
            if isinstance(ctr, tuple):
                sem, v = ctr
            else:
                sem, v = ctr.sem, (ctr.v if val is None else val)
            if v > 0:
                em(eng, lambda e, s=sem, v=v: e.wait_ge(s, v))

        def op(eng, fn, ctr=None, n=1):
            if ctr is None or n is None:
                em(eng, fn)
                return None
            ctr.v += n
            em(eng, lambda e, fn=fn, s=ctr.sem, n=n: fn(e).then_inc(s, n))
            return ctr.v

        def fence_dve():
            op("vector", lambda e: e.tensor_copy(out=junk[:], in_=junk[:]), eDVE)

        def fence_act():
            op("scalar", lambda e: e.activation(out=junk[:], in_=junk[:],
                                                func=AF.Identity), eACT)

        bank_last = {i: None for i in range(8)}

        def bank_acquire(eng, i):
            if bank_last[i] is not None:
                wait(eng, bank_last[i])

        def bank_release(i, ctr, val):
            bank_last[i] = (ctr.sem, val)

        dbg_tensors = {}

        def dbg_dump(name, src_ap, shape, dt, dep_ctr, dep_val,
                     stall=("vector", "scalar", "tensor")):
            if not dbg:
                return
            dtn = nc.dram_tensor("dbg_" + name, shape, dt, kind="ExternalOutput").ap()
            dbg_tensors[name] = dtn
            wait("gpsimd", dep_ctr, dep_val)
            v = op("gpsimd", lambda e, dtn=dtn, src_ap=src_ap: e.dma_start(
                out=dtn, in_=src_ap), cDBG, 16)
            for en in stall:
                wait(en, cDBG, v)

        for fn in [
            lambda e: e.dma_start(out=ident_f[:], in_=d_ident),
            lambda e: e.dma_start(out=ones_col_f[:], in_=d_ones_col),
            lambda e: e.dma_start(out=ones_row_f[:], in_=d_ones_row),
            lambda e: e.dma_start(out=ln_g[:], in_=d_ln_g),
            lambda e: e.dma_start(out=ln_b[:], in_=d_ln_b),
            lambda e: e.dma_start(out=conv_w[:], in_=d_convw),
            lambda e: e.dma_start(out=conv_b[:], in_=d_convb),
            lambda e: e.dma_start(out=dt_b[:], in_=d_dtb),
            lambda e: e.dma_start(out=A_neg[:], in_=d_logA),
            lambda e: e.dma_start(out=D_fm[:], in_=d_Dp),
            lambda e: e.dma_start(out=b1_fm[:], in_=d_b1),
            lambda e: e.dma_start(out=bg_row_s[:], in_=d_bg),
        ]:
            op("sync", fn, cSM, 16)
        op("sync", lambda e: e.dma_start(
            out=Wg_sb[:].rearrange("p (ks e) -> p ks e", e=NE),
            in_=d_Wg.rearrange("(ks p) e -> p ks e", p=128),
            allow_slow_non_contiguous=True), cSM, 16)
        v_small = cSM.v

        op("gpsimd", lambda e: e.dma_start(out=sel16_b[:], in_=d_sel16), cCVsm, 16)
        op("gpsimd", lambda e: e.dma_start(out=sel8_b[:], in_=d_sel8), cCVsm, 16)
        op("gpsimd", lambda e: e.dma_start(out=b2_b[:], in_=d_b2), cCVsm, 16)
        op("gpsimd", lambda e: e.dma_start(
            out=Bw_b[:].rearrange("p (ks s) -> p ks s", s=NS),
            in_=d_Bw.rearrange("(ks p) s -> p ks s", p=128),
            allow_slow_non_contiguous=True), cCVsm, 16)
        op("gpsimd", lambda e: e.dma_start(
            out=Cw_b[:].rearrange("p (ks s) -> p ks s", s=NS),
            in_=d_Cw.rearrange("(ks p) s -> p ks s", p=128),
            allow_slow_non_contiguous=True), cCVsm, 16)
        v_cast_small = cCVsm.v

        wait("scalar", cSM, v_small)
        v_exp = op("scalar", lambda e: e.activation(out=A_neg[:], in_=A_neg[:],
                                                    func=AF.Exp), eACT)
        wait("vector", eACT, v_exp)
        op("vector", lambda e: e.tensor_copy(out=ones_col_b[:], in_=ones_col_f[:]), eDVE)
        op("vector", lambda e: e.memset(h_state[:], 0.0), eDVE)
        op("vector", lambda e: e.memset(hist[:], 0.0), eDVE)
        op("vector", lambda e: e.memset(junk[:], 0.0), eDVE)
        op("vector", lambda e: e.tensor_scalar(out=A_neg[:], in0=A_neg[:],
                                               scalar1=-1.0, scalar2=None,
                                               op0=ALU.mult), eDVE)
        v_dset = eDVE.v
        wait("tensor", cSM, v_small)
        bank_acquire("tensor", 4)
        v_bgmm = op("tensor", lambda e: e.matmul(pb[4][:, 0:NE], ones_row_f[0:1, :],
                                                 bg_row_s[0:1, :],
                                                 start=True, stop=True), ePE, 1)
        wait("vector", ePE, v_bgmm)
        fence_dve()
        op("vector", lambda e: e.tensor_copy(out=bg_bcast[:], in_=pb[4][:, 0:NE]), eDVE)
        bank_release(4, eDVE, eDVE.v)

        for mt in range(16):
            src = d_Win.rearrange("(ks p) (mt m) -> mt p ks m", p=128, m=128)[mt]
            op("gpsimd", lambda e, src=src, mt=mt: e.dma_start(
                out=s_Win[mt].rearrange("p (ks m) -> p ks m", m=128), in_=src,
                allow_slow_non_contiguous=True), cCVF, 16)
        for mt in range(8):
            src = d_dtw.rearrange("(ks p) (mt m) -> mt p ks m", p=128, m=128)[mt]
            op("gpsimd", lambda e, src=src, mt=mt: e.dma_start(
                out=s_dtw[mt].rearrange("p (ks m) -> p ks m", m=128), in_=src,
                allow_slow_non_contiguous=True), cCVdt, 16)
        for mt in range(8):
            src = d_Wout.rearrange("(ks p) (mt m) -> mt p ks m", p=128, m=128)[mt]
            op("gpsimd", lambda e, src=src, mt=mt: e.dma_start(
                out=s_Wout[mt].rearrange("p (ks m) -> p ks m", m=128), in_=src,
                allow_slow_non_contiguous=True), cCVwo, 16)
        v_cv_front = cCVF.v
        v_cv_dt = cCVdt.v
        v_cv_wo = cCVwo.v
        v_cv_e = []
        for ee in range(NE):
            for hu in range(16):
                src = d_W1[ee].rearrange("(ks p) (hu h) -> hu p ks h", p=128, h=256)[hu]
                op("gpsimd", lambda e, src=src, ee=ee, hu=hu: e.dma_start(
                    out=s_W1[ee, hu].rearrange("p (ks h) -> p ks h", h=256), in_=src,
                    allow_slow_non_contiguous=True), cCVe[ee], 16)
            for dt in range(8):
                for hq in range(4):
                    src = d_W2[ee].rearrange(
                        "(hq hs p) (dt d) -> dt hq p hs d", hq=4, hs=8, p=128,
                        d=128)[dt, hq]
                    dst = s_W2[ee, dt].rearrange("p (hq hs d) -> hq p hs d",
                                                 hq=4, hs=8)[hq]
                    op("gpsimd", lambda e, src=src, dst=dst, ee=ee: e.dma_start(
                        out=dst, in_=src, allow_slow_non_contiguous=True), cCVe[ee], 16)
            v_cv_e.append((cCVe[ee].sem, cCVe[ee].v))

        xload_v = {}
        xload_consumed = {}

        def load_x_tile(gt):
            if gt - 2 in xload_consumed:
                wait("sync", ePE, xload_consumed[gt - 2])
            sl = (gt % 2) * 1024
            c_ = cXs[gt % 2]
            v = op("sync", lambda e, gt=gt, sl=sl: e.dma_start(
                out=x_tm[:, sl:sl + 1024], in_=dx[gt * 128:(gt + 1) * 128, :]), c_, 16)
            xload_v[gt] = (c_.sem, v)

        fw_state = {"count": 0, "loaded": {}, "consumed": {}}

        def stream_front_unit(scr_ap):
            u = fw_state["count"]
            if u >= 3:
                wait("sync", ePE, fw_state["consumed"][u - 3])
            sl = (u % 3) * 1024
            c_ = cFWs[u % 3]
            v = op("sync", lambda e, scr=scr_ap, sl=sl: e.dma_start(
                out=fw[:, sl:sl + 1024], in_=scr), c_, 16)
            fw_state["loaded"][u] = (c_.sem, v)
            fw_state["count"] = u + 1
            return u, sl

        chunk_S = {}
        pass_S = {}
        scan_state = {"v_mul_last": None}

        def fm_ln_stats_apply(src, ncols, writer, v_src_dve):
            op("vector", lambda e: e.tensor_copy(out=xsq_b[:, 0:NJ * ncols],
                                                 in_=src[:, 0:NJ * ncols]), eDVE)
            v_c1 = eDVE.v
            wait("scalar", eDVE, v_src_dve)
            op("scalar", lambda e: e.activation(out=z_silu[:, 0:NJ * ncols],
                                                in_=src[:, 0:NJ * ncols],
                                                func=AF.Square), eACT)
            v_c2 = eACT.v
            wait("tensor", eDVE, v_c1)
            wait("tensor", eACT, v_c2)
            bank_acquire("tensor", 4)
            bank_acquire("tensor", 5)
            for j in range(NJ):
                op("tensor", lambda e, j=j: e.matmul(
                    pb[4][0:1, 0:ncols], ones_col_b[:],
                    xsq_b[:, j * ncols:(j + 1) * ncols],
                    start=(j == 0), stop=(j == NJ - 1)), ePE, 1 if j == NJ - 1 else None)
            for j in range(NJ):
                op("tensor", lambda e, j=j: e.matmul(
                    pb[5][0:1, 0:ncols], ones_col_b[:],
                    z_silu[:, j * ncols:(j + 1) * ncols],
                    start=(j == 0), stop=(j == NJ - 1)), ePE, 1 if j == NJ - 1 else None)
            v_stat = ePE.v
            wait("vector", ePE, v_stat)
            fence_dve()
            op("vector", lambda e: e.tensor_scalar(out=mu_row[0:1, 0:ncols],
                                                   in0=pb[4][0:1, 0:ncols],
                                                   scalar1=1.0 / D, scalar2=None,
                                                   op0=ALU.mult), eDVE)
            op("vector", lambda e: e.tensor_mul(out=tmp_row[0:1, 0:ncols],
                                                in0=mu_row[0:1, 0:ncols],
                                                in1=mu_row[0:1, 0:ncols]), eDVE)
            op("vector", lambda e: e.scalar_tensor_tensor(
                out=var_row[0:1, 0:ncols], in0=pb[5][0:1, 0:ncols], scalar=1.0 / D,
                in1=tmp_row[0:1, 0:ncols], op0=ALU.mult, op1=ALU.subtract), eDVE)
            op("vector", lambda e: e.tensor_scalar(
                out=var_row[0:1, 0:ncols], in0=var_row[0:1, 0:ncols], scalar1=LN_EPS,
                scalar2=None, op0=ALU.add), eDVE)
            fence_dve()
            v_var = eDVE.v
            wait("scalar", eDVE, v_var)
            op("scalar", lambda e: e.activation(out=tmp_row[0:1, 0:ncols],
                                                in_=var_row[0:1, 0:ncols],
                                                func=AF.Ln), eACT)
            op("scalar", lambda e: e.activation(out=rstd_row[0:1, 0:ncols],
                                                in_=tmp_row[0:1, 0:ncols],
                                                func=AF.Exp, scale=-0.5), eACT)
            fence_act()
            v_rstd = eACT.v
            wait("tensor", eACT, v_rstd)
            op("tensor", lambda e: e.matmul(pb[4][:, 0:ncols], ones_row_f[0:1, :],
                                            mu_row[0:1, 0:ncols],
                                            start=True, stop=True), ePE)
            op("tensor", lambda e: e.matmul(pb[5][:, 0:ncols], ones_row_f[0:1, :],
                                            rstd_row[0:1, 0:ncols],
                                            start=True, stop=True), ePE, 1)
            v_bcst = ePE.v
            wait("vector", ePE, v_bcst)
            for j in range(NJ):
                writer(j)
            bank_release(4, eDVE, eDVE.v)
            bank_release(5, eDVE, eDVE.v)

        def make_front_stages(c):
            cs = chunk_S.setdefault(c, {})
            half = c % 2
            p_owner = c // 2
            stages = []

            def s_load():
                for tt in range(2):
                    load_x_tile(2 * c + tt)
            stages.append(s_load)

            def s_transpose():
                for tt in range(2):
                    gt = 2 * c + tt
                    sl = (gt % 2) * 1024
                    wait("tensor", xload_v[gt])
                    if c == 0 and tt == 0:
                        wait("tensor", cSM, v_small)
                    bank_acquire("tensor", 4)
                    bank_acquire("tensor", 5)
                    for j in range(NJ):
                        bank = pb[4] if j < 4 else pb[5]
                        colo = (j % 4) * 128
                        op("tensor", lambda e, sl=sl, j=j, bank=bank, colo=colo:
                           e.transpose(bank[:, colo:colo + 128],
                                       x_tm[:, sl + j * 128: sl + (j + 1) * 128],
                                       ident_f[:]), ePE, 1 if j == NJ - 1 else None)
                    v_tr = ePE.v
                    xload_consumed[gt] = v_tr
                    wait("vector", ePE, v_tr)
                    for bi, bank in ((0, pb[4]), (1, pb[5])):
                        dst = x_fm[:].rearrange("p (j t) -> p j t", j=NJ)[
                            :, bi * 4:(bi + 1) * 4, tt * 128:(tt + 1) * 128]
                        srcv = bank[:].rearrange("p (j t) -> p j t", j=4)
                        op("vector", lambda e, dst=dst, srcv=srcv: e.tensor_copy(
                            out=dst, in_=srcv), eDVE, 1)
                    bank_release(4, eDVE, eDVE.v)
                    bank_release(5, eDVE, eDVE.v)
                cs["v_xfm"] = eDVE.v
            stages.append(s_transpose)

            def s_ln1():
                if c == 0:
                    wait("vector", eDVE, v_dset)

                def writer(j):
                    op("vector", lambda e, j=j: e.tensor_sub(
                        out=hc[:, 0:LC], in0=x_fm[:, j * LC:(j + 1) * LC],
                        in1=pb[4][:, 0:LC]), eDVE)
                    op("vector", lambda e: e.tensor_mul(
                        out=hc[:, LC:2 * LC], in0=hc[:, 0:LC], in1=pb[5][:, 0:LC]),
                        eDVE)
                    op("vector", lambda e, j=j: e.tensor_scalar(
                        out=xn_b[:, j * LC:(j + 1) * LC], in0=hc[:, LC:2 * LC],
                        scalar1=ln_g[:, j:j + 1], scalar2=ln_b[:, j:j + 1],
                        op0=ALU.mult, op1=ALU.add), eDVE)
                fm_ln_stats_apply(x_fm, LC, writer, cs["v_xfm"])
                cs["v_xn"] = eDVE.v
            stages.append(s_ln1)

            def s_win(mt0, mt1):
                for mt in range(mt0, mt1):
                    if c == 0 and mt == 0:
                        wait("sync", cCVF, v_cv_front)
                    u, sl = stream_front_unit(s_Win[mt])
                    wait("tensor", fw_state["loaded"][u])
                    if mt == mt0:
                        wait("tensor", eDVE, cs["v_xn"])
                    bank_i = 4 + (mt % 4)
                    bank = pb[bank_i]
                    bank_acquire("tensor", bank_i)
                    for ks in range(8):
                        op("tensor", lambda e, sl=sl, ks=ks, bank=bank: e.matmul(
                            bank[:, 0:LC], fw[:, sl + ks * 128: sl + (ks + 1) * 128],
                            xn_b[:, ks * LC:(ks + 1) * LC],
                            start=(ks == 0), stop=(ks == 7)), ePE, 1 if ks == 7 else None)
                    fw_state["consumed"][u] = ePE.v
                    v_mm = ePE.v
                    if mt < 8:
                        j = mt
                        wait("vector", ePE, v_mm)
                        op("vector", lambda e, j=j: e.tensor_copy(
                            out=pre[:, 0:3], in_=hist[:, j * 3:(j + 1) * 3]), eDVE)
                        op("vector", lambda e, bank=bank: e.tensor_copy(
                            out=pre[:, 3:3 + LC], in_=bank[:, 0:LC]), eDVE)
                        bank_release(bank_i, eDVE, eDVE.v)
                        if cs.get("v_convsilu_prev") is not None:
                            wait("vector", eACT, cs["v_convsilu_prev"])
                        op("vector", lambda e, j=j: e.tensor_scalar(
                            out=sp1[:], in0=pre[:, 0:LC],
                            scalar1=conv_w[:, j * KCONV:j * KCONV + 1], scalar2=None,
                            op0=ALU.mult), eDVE)
                        for k in range(1, KCONV):
                            op("vector", lambda e, k=k, j=j: e.scalar_tensor_tensor(
                                out=sp1[:], in0=pre[:, k:k + LC],
                                scalar=conv_w[:, j * KCONV + k:j * KCONV + k + 1],
                                in1=sp1[:], op0=ALU.mult, op1=ALU.add), eDVE)
                        op("vector", lambda e, j=j: e.tensor_copy(
                            out=hist[:, j * 3:(j + 1) * 3], in_=pre[:, LC:LC + 3]),
                            eDVE)
                        v_conv = eDVE.v
                        wait("scalar", eDVE, v_conv)
                        op("scalar", lambda e, j=j: e.activation(
                            out=x_ssm[:, j * LC:(j + 1) * LC], in_=sp1[:],
                            func=AF.Silu, bias=conv_b[:, j:j + 1]), eACT)
                        cs["v_convsilu_prev"] = eACT.v
                    else:
                        j = mt - 8
                        wait("scalar", ePE, v_mm)
                        op("scalar", lambda e, j=j, bank=bank: e.activation(
                            out=z_silu[:, j * LC:(j + 1) * LC], in_=bank[:, 0:LC],
                            func=AF.Silu), eACT)
                        bank_release(bank_i, eACT, eACT.v)
                if mt1 == 16:
                    v_front_act = eACT.v
                    wait("vector", eACT, v_front_act)
                    op("vector", lambda e: e.tensor_copy(out=x_ssm_b[:], in_=x_ssm[:]),
                       eDVE)
                    cs["v_xssm_b"] = eDVE.v
            stages.append(lambda: s_win(0, 8))
            stages.append(lambda: s_win(8, 16))

            def s_dt(j0, j1):
                for j in range(j0, j1):
                    if c == 0 and j == 0:
                        wait("sync", cCVdt, v_cv_dt)
                    u, sl = stream_front_unit(s_dtw[j])
                    wait("tensor", fw_state["loaded"][u])
                    if j == j0:
                        wait("tensor", eDVE, cs["v_xssm_b"])
                    v_sp_war = cs.get("v_du_prev")
                    bank_i = 4 + (j % 4)
                    bank = pb[bank_i]
                    bank_acquire("tensor", bank_i)
                    for ks in range(8):
                        op("tensor", lambda e, sl=sl, ks=ks, bank=bank: e.matmul(
                            bank[:, 0:LC], fw[:, sl + ks * 128: sl + (ks + 1) * 128],
                            x_ssm_b[:, ks * LC:(ks + 1) * LC],
                            start=(ks == 0), stop=(ks == 7)), ePE, 1 if ks == 7 else None)
                    fw_state["consumed"][u] = ePE.v
                    v_mm = ePE.v
                    wait("scalar", ePE, v_mm)
                    if v_sp_war is not None:
                        wait("scalar", eDVE, v_sp_war)
                    op("scalar", lambda e, j=j, bank=bank: e.activation(
                        out=sp1[:], in_=bank[:, 0:LC], func=AF.Abs,
                        bias=dt_b[:, j:j + 1]), eACT)
                    op("scalar", lambda e: e.activation(out=sp1[:], in_=sp1[:],
                                                        func=AF.Exp, scale=-1.0), eACT)
                    op("scalar", lambda e: e.activation(out=sp1[:], in_=sp1[:],
                                                        func=AF.Ln, bias=1.0), eACT)
                    op("scalar", lambda e, j=j, bank=bank: e.activation(
                        out=sp2[:], in_=bank[:, 0:LC], func=AF.Relu,
                        bias=dt_b[:, j:j + 1]), eACT)
                    v_sp = eACT.v
                    bank_release(bank_i, eACT, v_sp)
                    wait("vector", eACT, v_sp)
                    op("vector", lambda e: e.tensor_add(out=dt_j[:], in0=sp1[:],
                                                        in1=sp2[:]), eDVE)
                    op("vector", lambda e, j=j: e.tensor_mul(
                        out=u_buf[:, j * LC:(j + 1) * LC], in0=dt_j[:],
                        in1=x_ssm[:, j * LC:(j + 1) * LC]), eDVE)
                    op("vector", lambda e, j=j: e.tensor_scalar(
                        out=sp2[:], in0=dt_j[:], scalar1=A_neg[:, j:j + 1],
                        scalar2=None, op0=ALU.mult), eDVE)
                    v_du = eDVE.v
                    cs["v_du_prev"] = v_du
                    wait("scalar", eDVE, v_du)
                    op("scalar", lambda e, j=j: e.activation(
                        out=decay[:, j * LC:(j + 1) * LC], in_=sp2[:], func=AF.Exp),
                        eACT)
                if j1 == NJ:
                    cs["v_decay"] = eACT.v
            stages.append(lambda: s_dt(0, 4))
            stages.append(lambda: s_dt(4, 8))

            def s_bc():
                wait("tensor", eDVE, cs["v_xssm_b"])
                if c == 0:
                    wait("tensor", cCVsm, v_cast_small)
                bank_acquire("tensor", 4)
                for w_i, wsb in ((0, Bw_b), (1, Cw_b)):
                    for tt in range(2):
                        for ks in range(8):
                            op("tensor", lambda e, w_i=w_i, wsb=wsb, tt=tt, ks=ks:
                               e.matmul(
                                   pb[4][:, (w_i * 2 + tt) * NS:
                                         (w_i * 2 + tt + 1) * NS],
                                   x_ssm_b[:, ks * LC + tt * 128:
                                           ks * LC + tt * 128 + 128],
                                   wsb[:, ks * NS:(ks + 1) * NS],
                                   start=(ks == 0), stop=(ks == 7)),
                               ePE, 1 if (w_i == 1 and tt == 1 and ks == 7) else None)
                v_bcmm = ePE.v
                wait("vector", ePE, v_bcmm)
                fence_dve()
                op("vector", lambda e: e.tensor_copy(out=Btm[:],
                                                     in_=pb[4][:, 0:2 * NS]), eDVE)
                op("vector", lambda e: e.tensor_copy(out=Ctm[:],
                                                     in_=pb[4][:, 2 * NS:4 * NS]), eDVE)
                bank_release(4, eDVE, eDVE.v)
                fence_dve()
                v_bctm = eDVE.v
                wait("tensor", eDVE, v_bctm)
                bank_acquire("tensor", 5)
                for w_i, wtm in ((0, Btm), (1, Ctm)):
                    for tt in range(2):
                        op("tensor", lambda e, w_i=w_i, wtm=wtm, tt=tt: e.transpose(
                            pb[5][0:NS, (w_i * 2 + tt) * 128:(w_i * 2 + tt + 1) * 128],
                            wtm[:, tt * NS:(tt + 1) * NS], ident_f[:]),
                            ePE, 1 if (w_i == 1 and tt == 1) else None)
                v_bcT = ePE.v
                wait("vector", ePE, v_bcT)
                fence_dve()
                op("vector", lambda e: e.tensor_copy(out=Bfm_b[:],
                                                     in_=pb[5][0:NS, 0:2 * 128]), eDVE)
                op("vector", lambda e: e.tensor_copy(out=Cfm_b[:],
                                                     in_=pb[5][0:NS, 2 * 128:4 * 128]),
                   eDVE)
                bank_release(5, eDVE, eDVE.v)
                fence_dve()
                cs["v_bcfm"] = eDVE.v
            stages.append(s_bc)

            def s_scan(g, jh):
                if jh == 0:
                    if g == 0:
                        wait("tensor", eDVE, cs["v_bcfm"])
                    for bank_i in (4, 5, 6, 7):
                        bank_acquire("tensor", bank_i)
                    for i in range(SG):
                        s_ = g * SG + i
                        bank = pb[6] if i < 2 else pb[7]
                        colo = (i % 2) * LC
                        op("tensor", lambda e, s_=s_, bank=bank, colo=colo: e.matmul(
                            bank[:, colo:colo + LC],
                            sel16_b[0:NS, s_ * 128:(s_ + 1) * 128],
                            Cfm_b[0:NS, :], start=True, stop=True),
                            ePE, 1 if i == SG - 1 else None)
                    for i in range(SG):
                        s_ = g * SG + i
                        bank = pb[4] if i < 2 else pb[5]
                        colo = (i % 2) * LC
                        op("tensor", lambda e, s_=s_, bank=bank, colo=colo: e.matmul(
                            bank[:, colo:colo + LC],
                            sel16_b[0:NS, s_ * 128:(s_ + 1) * 128],
                            Bfm_b[0:NS, :], start=True, stop=True),
                            ePE, 1 if i == SG - 1 else None)
                    cs["v_bg"] = ePE.v
                    wait("scalar", ePE, cs["v_bg"])
                    if scan_state["v_mul_last"] is not None:
                        wait("scalar", eDVE, scan_state["v_mul_last"])
                    op("scalar", lambda e: e.activation(
                        out=B_sb[:, 0:2 * LC], in_=pb[4][:, 0:2 * LC],
                        func=AF.Identity), eACT)
                    op("scalar", lambda e: e.activation(
                        out=B_sb[:, 2 * LC:4 * LC], in_=pb[5][:, 0:2 * LC],
                        func=AF.Identity), eACT)
                    op("scalar", lambda e: e.activation(
                        out=C_sb[:, 0:2 * LC], in_=pb[6][:, 0:2 * LC],
                        func=AF.Identity), eACT)
                    op("scalar", lambda e: e.activation(
                        out=C_sb[:, 2 * LC:4 * LC], in_=pb[7][:, 0:2 * LC],
                        func=AF.Identity), eACT)
                    cs["v_bcsb"] = eACT.v
                    for bank_i in (4, 5, 6, 7):
                        bank_release(bank_i, eACT, cs["v_bcsb"])
                    wait("vector", eACT, cs["v_bcsb"])
                for j in range(jh * 4, jh * 4 + 4):
                    for i in range(SG):
                        s_ = g * SG + i
                        op("vector", lambda e, j=j, i=i: e.tensor_mul(
                            out=d1b[:], in0=u_buf[:, j * LC:(j + 1) * LC],
                            in1=B_sb[:, i * LC:(i + 1) * LC]), eDVE)
                        op("vector", lambda e, j=j, i=i, s_=s_: e.tensor_tensor_scan(
                            out=h_traj[:, i * LC:(i + 1) * LC],
                            data0=decay[:, j * LC:(j + 1) * LC], data1=d1b[:],
                            initial=h_state[:, j * NS + s_:j * NS + s_ + 1],
                            op0=ALU.mult, op1=ALU.add), eDVE)
                    for i in range(SG):
                        op("vector", lambda e, i=i: e.tensor_mul(
                            out=hcb[:, i * LC:(i + 1) * LC],
                            in0=h_traj[:, i * LC:(i + 1) * LC],
                            in1=C_sb[:, i * LC:(i + 1) * LC]), eDVE)
                    scan_state["v_mul_last"] = eDVE.v
                    for i in range(SG):
                        op("vector", lambda e, j=j, g=g, i=i: e.tensor_copy(
                            out=h_state[:, j * NS + g * SG + i:
                                        j * NS + g * SG + i + 1],
                            in_=h_traj[:, i * LC + LC - 1:i * LC + LC]), eDVE)
                    op("vector", lambda e: e.tensor_add(
                        out=hcb[:, 0:LC], in0=hcb[:, 0:LC], in1=hcb[:, LC:2 * LC]),
                        eDVE)
                    op("vector", lambda e: e.tensor_add(
                        out=hcb[:, 2 * LC:3 * LC], in0=hcb[:, 2 * LC:3 * LC],
                        in1=hcb[:, 3 * LC:4 * LC]), eDVE)
                    if g == 0:
                        op("vector", lambda e, j=j: e.tensor_add(
                            out=y_buf[:, j * LC:(j + 1) * LC],
                            in0=hcb[:, 0:LC], in1=hcb[:, 2 * LC:3 * LC]), eDVE)
                    else:
                        op("vector", lambda e: e.tensor_add(
                            out=d1s[:], in0=hcb[:, 0:LC], in1=hcb[:, 2 * LC:3 * LC]),
                            eDVE)
                        op("vector", lambda e, j=j: e.tensor_add(
                            out=y_buf[:, j * LC:(j + 1) * LC],
                            in0=y_buf[:, j * LC:(j + 1) * LC], in1=d1s[:]), eDVE)
            for g in range(4):
                stages.append(lambda g=g: s_scan(g, 0))
                stages.append(lambda g=g: s_scan(g, 1))

            def s_y():
                for j in range(NJ):
                    op("vector", lambda e, j=j: e.scalar_tensor_tensor(
                        out=y_buf[:, j * LC:(j + 1) * LC],
                        in0=x_ssm[:, j * LC:(j + 1) * LC],
                        scalar=D_fm[:, j:j + 1], in1=y_buf[:, j * LC:(j + 1) * LC],
                        op0=ALU.mult, op1=ALU.add), eDVE)
                    op("vector", lambda e, j=j: e.tensor_mul(
                        out=xn_b[:, j * LC:(j + 1) * LC],
                        in0=y_buf[:, j * LC:(j + 1) * LC],
                        in1=z_silu[:, j * LC:(j + 1) * LC]), eDVE)
                cs["v_yb"] = eDVE.v
            stages.append(s_y)

            def s_wout():
                for mt in range(8):
                    if c == 0 and mt == 0:
                        wait("sync", cCVwo, v_cv_wo)
                    u, sl = stream_front_unit(s_Wout[mt])
                    wait("tensor", fw_state["loaded"][u])
                    if mt == 0:
                        wait("tensor", eDVE, cs["v_yb"])
                    bank_i = 4 + (mt % 4)
                    bank = pb[bank_i]
                    bank_acquire("tensor", bank_i)
                    for ks in range(8):
                        op("tensor", lambda e, sl=sl, ks=ks, bank=bank: e.matmul(
                            bank[:, 0:LC], fw[:, sl + ks * 128: sl + (ks + 1) * 128],
                            xn_b[:, ks * LC:(ks + 1) * LC],
                            start=(ks == 0), stop=(ks == 7)), ePE, 1 if ks == 7 else None)
                    fw_state["consumed"][u] = ePE.v
                    v_mm = ePE.v
                    wait("vector", ePE, v_mm)
                    op("vector", lambda e, mt=mt, bank=bank: e.scalar_tensor_tensor(
                        out=x_fm[:, mt * LC:(mt + 1) * LC],
                        in0=x_fm[:, mt * LC:(mt + 1) * LC], scalar=2.0,
                        in1=bank[:, 0:LC], op0=ALU.mult, op1=ALU.add), eDVE)
                    bank_release(bank_i, eDVE, eDVE.v)
                cs["v_s1"] = eDVE.v
            stages.append(s_wout)

            def s_ln2():
                if (p_owner - 1) in pass_S:
                    wait("vector", eDVE, pass_S[p_owner - 1]["v_oinit"])

                def writer(j):
                    xo = j * SC + half * LC
                    op("vector", lambda e, j=j: e.tensor_sub(
                        out=hc[:, 0:LC], in0=x_fm[:, j * LC:(j + 1) * LC],
                        in1=pb[4][:, 0:LC]), eDVE)
                    op("vector", lambda e, xo=xo: e.tensor_mul(
                        out=x2[:, xo:xo + LC], in0=hc[:, 0:LC], in1=pb[5][:, 0:LC]),
                        eDVE)
                fm_ln_stats_apply(x_fm, LC, writer, cs["v_s1"])
                cs["v_x2"] = eDVE.v
            stages.append(s_ln2)

            def s_x2b():
                if (p_owner - 1) in pass_S:
                    ps_prev = pass_S[p_owner - 1]
                    assert "v_ln3stat" in ps_prev, "x2b before LN3 stats"
                    wait("vector", ePE, ps_prev["v_ln3stat"])
                for j in range(NJ):
                    xo = j * SC + half * LC
                    op("vector", lambda e, xo=xo: e.tensor_copy(
                        out=x2b[:, xo:xo + LC], in_=x2[:, xo:xo + LC]), eDVE)
                cs["v_x2b"] = eDVE.v
            return stages, s_x2b

        w1s = {"count": 0, "loaded": {}, "consumed": {}}
        w2s = {"count": 0, "loaded": {}, "consumed": {}}
        out_dma_v = {}

        def make_moe_pass(p):
            ps = pass_S.setdefault(p, {})
            ps["v_oadd"] = {}
            ps["ounit"] = [0]
            tok0 = p * SC
            w2rot = [2, 3, 0, 1]

            def b_gate():
                wait("tensor", eDVE, chunk_S[2 * p + 1]["v_x2"])
                bank_acquire("tensor", 0)
                for tt in range(4):
                    for ks in range(8):
                        op("tensor", lambda e, tt=tt, ks=ks: e.matmul(
                            pb[0][:, tt * NE:(tt + 1) * NE],
                            x2[:, ks * SC + tt * 128: ks * SC + tt * 128 + 128],
                            Wg_sb[:, ks * NE:(ks + 1) * NE],
                            start=(ks == 0), stop=(ks == 7)),
                            ePE, 1 if (tt == 3 and ks == 7) else None)
                v_gmm = ePE.v
                wait("vector", ePE, v_gmm)
                fence_dve()
                for tt in range(4):
                    op("vector", lambda e, tt=tt: e.tensor_add(
                        out=gate_tm[:, tt * NE:(tt + 1) * NE],
                        in0=pb[0][:, tt * NE:(tt + 1) * NE], in1=bg_bcast[:]), eDVE)
                bank_release(0, eDVE, eDVE.v)
                for tt in range(4):
                    op("vector", lambda e, tt=tt: e.reduce_max(
                        out=gsc[:, tt:tt + 1], in_=gate_tm[:, tt * NE:(tt + 1) * NE],
                        axis=AX.X), eDVE)
                fence_dve()
                op("vector", lambda e: e.tensor_scalar(
                    out=gsc[:, 0:4], in0=gsc[:, 0:4], scalar1=-1.0, scalar2=None,
                    op0=ALU.mult), eDVE)
                fence_dve()
                v_m = eDVE.v
                wait("scalar", eDVE, v_m)
                for tt in range(4):
                    go = tt * NE
                    op("scalar", lambda e, go=go, tt=tt: e.activation(
                        out=probs[:, go:go + NE], in_=gate_tm[:, go:go + NE],
                        func=AF.Exp, bias=gsc[:, tt:tt + 1]), eACT)
                fence_act()
                v_e = eACT.v
                wait("vector", eACT, v_e)
                for tt in range(4):
                    op("vector", lambda e, tt=tt: e.reduce_sum(
                        out=gsc[:, 4 + tt:5 + tt],
                        in_=probs[:, tt * NE:(tt + 1) * NE], axis=AX.X), eDVE)
                fence_dve()
                op("vector", lambda e: e.reciprocal(out=gsc[:, 4:8], in_=gsc[:, 4:8]),
                   eDVE)
                fence_dve()
                for tt in range(4):
                    go = tt * NE
                    op("vector", lambda e, go=go, tt=tt: e.tensor_scalar(
                        out=probs[:, go:go + NE], in0=probs[:, go:go + NE],
                        scalar1=gsc[:, 4 + tt:5 + tt], scalar2=None, op0=ALU.mult),
                        eDVE)
                fence_dve()
                for tt in range(4):
                    op("vector", lambda e, tt=tt: e.reduce_max(
                        out=w_tm[:, tt:tt + 1], in_=probs[:, tt * NE:(tt + 1) * NE],
                        axis=AX.X), eDVE)
                fence_dve()
                for tt in range(4):
                    go = tt * NE
                    op("vector", lambda e, go=go, tt=tt: e.tensor_scalar(
                        out=mask_tm[:, go:go + NE], in0=probs[:, go:go + NE],
                        scalar1=w_tm[:, tt:tt + 1], scalar2=None, op0=ALU.is_ge), eDVE)
                fence_dve()
                for tt in range(4):
                    go = tt * NE
                    op("vector", lambda e, go=go, tt=tt: e.tensor_scalar(
                        out=mask_tm[:, go:go + NE], in0=mask_tm[:, go:go + NE],
                        scalar1=w_tm[:, tt:tt + 1], scalar2=None, op0=ALU.mult), eDVE)
                fence_dve()
                v_mask = eDVE.v
                if dbg and p == 0:
                    dbg_dump("probs", probs[:], [128, 4 * NE], F32, eDVE, v_mask)
                    dbg_dump("maskwtm", mask_tm[:], [128, 4 * NE], F32, eDVE, v_mask)
                wait("tensor", eDVE, v_mask)
                bank_acquire("tensor", 1)
                for tt in range(4):
                    op("tensor", lambda e, tt=tt: e.transpose(
                        pb[1][0:NE, tt * 128:(tt + 1) * 128],
                        mask_tm[:, tt * NE:(tt + 1) * NE], ident_f[:]),
                        ePE, 1 if tt == 3 else None)
                v_maskT = ePE.v
                wait("vector", ePE, v_maskT)
                fence_dve()
                op("vector", lambda e: e.tensor_copy(out=maskw_fm_b[:],
                                                     in_=pb[1][0:NE, 0:SC]), eDVE)
                bank_release(1, eDVE, eDVE.v)
                fence_dve()
                v_maskf = eDVE.v
                wait("tensor", eDVE, v_maskf)
                for dt in range(8):
                    bank_i = dt % 2
                    bank = pb[bank_i]
                    bank_acquire("tensor", bank_i)
                    op("tensor", lambda e, dt=dt, bank=bank: e.matmul(
                        bank[:, 0:SC], b2_b[0:NE, dt * 128:(dt + 1) * 128],
                        maskw_fm_b[0:NE, :], start=True, stop=True), ePE, 1)
                    v_b2 = ePE.v
                    wait("vector", ePE, v_b2)
                    op("vector", lambda e, dt=dt, bank=bank: e.tensor_add(
                        out=o_acc[:, dt * SC:(dt + 1) * SC],
                        in0=x2[:, dt * SC:(dt + 1) * SC], in1=bank[:, 0:SC]), eDVE, 1)
                    bank_release(bank_i, eDVE, eDVE.v)
                ps["v_oinit"] = eDVE.v

            def b_w1(ee, huh):
                if huh == 0:
                    bank_acquire("tensor", 2)
                    op("tensor", lambda e, ee=ee: e.matmul(
                        pb[2][:, 0:SC], sel8_b[0:NE, ee * 128:(ee + 1) * 128],
                        maskw_fm_b[0:NE, :], start=True, stop=True), ePE, 1)
                    v_mb = ePE.v
                    wait("vector", ePE, v_mb)
                    op("vector", lambda e: e.tensor_copy(out=mask_b[:],
                                                         in_=pb[2][:, 0:SC]), eDVE)
                    bank_release(2, eDVE, eDVE.v)
                    ps["v_mbs"] = eDVE.v
                for hu in range(huh * 8, huh * 8 + 8):
                    u = w1s["count"]
                    if u >= 3:
                        wait("sync", ePE, w1s["consumed"][u - 3])
                    sl = (u % 3) * 2048
                    if p == 0 and hu == 0 and huh == 0:
                        wait("sync", v_cv_e[ee])
                    c_ = cW1s[u % 3]
                    v = op("sync", lambda e, ee=ee, hu=hu, sl=sl: e.dma_start(
                        out=w1r[:, sl:sl + 2048], in_=s_W1[ee, hu]), c_, 16)
                    w1s["loaded"][u] = (c_.sem, v)
                    w1s["count"] = u + 1
                    wait("tensor", w1s["loaded"][u])
                    if hu == 0 and huh == 0:
                        wait("tensor", eDVE, chunk_S[2 * p + 1]["v_x2b"])
                    for ht in range(2):
                        hidx = hu * 2 + ht
                        bank_i = hidx % 4
                        bank = pb[bank_i]
                        bank_acquire("tensor", bank_i)
                        for ks in range(8):
                            op("tensor", lambda e, sl=sl, ks=ks, ht=ht, bank=bank:
                               e.matmul(
                                   bank[:, 0:SC],
                                   w1r[:, sl + ks * 256 + ht * 128:
                                       sl + ks * 256 + ht * 128 + 128],
                                   x2b[:, ks * SC:(ks + 1) * SC],
                                   start=(ks == 0), stop=(ks == 7)),
                               ePE, 1 if ks == 7 else None)
                        v_mm = ePE.v
                        if ht == 1:
                            w1s["consumed"][u] = v_mm
                        if ee == NE - 1 and hidx == 31:
                            ps["v_w1_last"] = v_mm
                        wait("scalar", ePE, v_mm)
                        v_silu = op(
                            "scalar", lambda e, ee=ee, hidx=hidx, bank=bank:
                            e.activation(
                                out=h1[:, hidx * SC:(hidx + 1) * SC],
                                in_=bank[:, 0:SC], func=AF.Silu,
                                bias=b1_fm[:, ee * 32 + hidx:ee * 32 + hidx + 1]),
                            eACT, 1)
                        bank_release(bank_i, eACT, v_silu)
                if huh == 1:
                    ps["v_h1"] = eACT.v

            def b_w2(ee, dth):
                for dt in range(dth * 4, dth * 4 + 4):
                    if p == 0 and dt == 0 and dth == 0:
                        wait("sync", v_cv_e[ee])
                    gi = ps["ounit"][0]
                    bank_i = [2, 3, 0, 1][gi % 4]
                    bank = pb[bank_i]
                    bank_acquire("tensor", bank_i)
                    for hh in range(2):
                        u = w2s["count"]
                        if u >= 2:
                            wait("sync", ePE, w2s["consumed"][u - 2])
                        sl = (u % 2) * 2048
                        c_ = cW2s[u % 2]
                        v = op("sync", lambda e, ee=ee, dt=dt, hh=hh, sl=sl:
                               e.dma_start(
                                   out=w2r[:, sl:sl + 2048],
                                   in_=s_W2[ee, dt][:, hh * 2048:(hh + 1) * 2048]),
                               c_, 16)
                        w2s["loaded"][u] = (c_.sem, v)
                        w2s["count"] = u + 1
                        wait("tensor", w2s["loaded"][u])
                        if dt == 0 and dth == 0 and hh == 0:
                            wait("tensor", eACT, ps["v_h1"])
                        for hs in range(16):
                            hsg = hh * 16 + hs
                            op("tensor", lambda e, sl=sl, hs=hs, hsg=hsg, bank=bank:
                               e.matmul(
                                   bank[:, 0:SC],
                                   w2r[:, sl + hs * 128:sl + (hs + 1) * 128],
                                   h1[:, hsg * SC:(hsg + 1) * SC],
                                   start=(hsg == 0), stop=(hsg == 31)),
                               ePE, 1 if hs == 15 else None)
                        w2s["consumed"][u] = ePE.v
                    v_mm = ePE.v
                    wait("vector", ePE, v_mm)
                    if ee == 0 and dt == 0:
                        wait("vector", eDVE, ps["v_mbs"])
                    op("vector", lambda e, bank=bank: e.tensor_mul(
                        out=tmp_o[:], in0=bank[:, 0:SC], in1=mask_b[:]), eDVE)
                    vv = op("vector", lambda e, dt=dt: e.tensor_add(
                        out=o_acc[:, dt * SC:(dt + 1) * SC],
                        in0=o_acc[:, dt * SC:(dt + 1) * SC], in1=tmp_o[:]), eDVE, 1)
                    ps["v_oadd"][gi] = vv
                    bank_release(bank_i, eDVE, vv)
                    ps["ounit"][0] += 1

            def b_ln3out():
                v_s2 = ps["v_oadd"][ps["ounit"][0] - 1]
                if dbg and p == 0:
                    dbg_dump("oacc", o_acc[:], [128, NJ * SC], F32, eDVE, v_s2)
                op("vector", lambda e: e.tensor_copy(out=x2b[:], in_=o_acc[:]), eDVE)
                v_c1 = eDVE.v
                wait("scalar", eDVE, v_s2)
                op("scalar", lambda e: e.activation(out=h1[:, 0:NJ * SC], in_=o_acc[:],
                                                    func=AF.Square), eACT)
                v_c2 = eACT.v
                wait("tensor", eDVE, v_c1)
                wait("tensor", eACT, v_c2)
                bank_acquire("tensor", 0)
                bank_acquire("tensor", 1)
                for j in range(NJ):
                    op("tensor", lambda e, j=j: e.matmul(
                        pb[0][0:1, 0:SC], ones_col_b[:], x2b[:, j * SC:(j + 1) * SC],
                        start=(j == 0), stop=(j == NJ - 1)),
                        ePE, 1 if j == NJ - 1 else None)
                for j in range(NJ):
                    op("tensor", lambda e, j=j: e.matmul(
                        pb[1][0:1, 0:SC], ones_col_b[:], h1[:, j * SC:(j + 1) * SC],
                        start=(j == 0), stop=(j == NJ - 1)),
                        ePE, 1 if j == NJ - 1 else None)
                v_stat = ePE.v
                ps["v_ln3stat"] = v_stat
                wait("vector", ePE, v_stat)
                fence_dve()
                op("vector", lambda e: e.tensor_scalar(out=mu_row[0:1, 0:SC],
                                                       in0=pb[0][0:1, 0:SC],
                                                       scalar1=1.0 / D, scalar2=None,
                                                       op0=ALU.mult), eDVE)
                op("vector", lambda e: e.tensor_mul(out=tmp_row[0:1, 0:SC],
                                                    in0=mu_row[0:1, 0:SC],
                                                    in1=mu_row[0:1, 0:SC]), eDVE)
                op("vector", lambda e: e.scalar_tensor_tensor(
                    out=var_row[0:1, 0:SC], in0=pb[1][0:1, 0:SC], scalar=1.0 / D,
                    in1=tmp_row[0:1, 0:SC], op0=ALU.mult, op1=ALU.subtract), eDVE)
                op("vector", lambda e: e.tensor_scalar(
                    out=var_row[0:1, 0:SC], in0=var_row[0:1, 0:SC], scalar1=LN_EPS,
                    scalar2=None, op0=ALU.add), eDVE)
                fence_dve()
                v_var = eDVE.v
                wait("scalar", eDVE, v_var)
                op("scalar", lambda e: e.activation(out=tmp_row[0:1, 0:SC],
                                                    in_=var_row[0:1, 0:SC],
                                                    func=AF.Ln), eACT)
                op("scalar", lambda e: e.activation(out=rstd_row[0:1, 0:SC],
                                                    in_=tmp_row[0:1, 0:SC],
                                                    func=AF.Exp, scale=-0.5), eACT)
                fence_act()
                v_rstd = eACT.v
                wait("tensor", eACT, v_rstd)
                op("tensor", lambda e: e.matmul(pb[0][:, 0:SC], ones_row_f[0:1, :],
                                                mu_row[0:1, 0:SC],
                                                start=True, stop=True), ePE)
                op("tensor", lambda e: e.matmul(pb[1][:, 0:SC], ones_row_f[0:1, :],
                                                rstd_row[0:1, 0:SC],
                                                start=True, stop=True), ePE, 1)
                v_bcst = ePE.v
                wait("vector", ePE, v_bcst)
                for j in range(NJ):
                    op("vector", lambda e, j=j: e.tensor_sub(
                        out=tmp_o[:], in0=o_acc[:, j * SC:(j + 1) * SC],
                        in1=pb[0][:, 0:SC]), eDVE)
                    op("vector", lambda e, j=j: e.tensor_mul(
                        out=o_acc[:, j * SC:(j + 1) * SC], in0=tmp_o[:],
                        in1=pb[1][:, 0:SC]), eDVE)
                bank_release(0, eDVE, eDVE.v)
                bank_release(1, eDVE, eDVE.v)
                v_x3 = eDVE.v
                wait("tensor", eDVE, v_x3)
                for tt in range(4):
                    bank_acquire("tensor", 0)
                    bank_acquire("tensor", 1)
                    for j in range(NJ):
                        bank = pb[0] if j < 4 else pb[1]
                        colo = (j % 4) * 128
                        op("tensor", lambda e, j=j, tt=tt, bank=bank, colo=colo:
                           e.transpose(
                               bank[:, colo:colo + 128],
                               o_acc[:, j * SC + tt * 128: j * SC + tt * 128 + 128],
                               ident_f[:]), ePE, 1 if j == NJ - 1 else None)
                    v_trT = ePE.v
                    wait("vector", ePE, v_trT)
                    prev = (p, tt - 1) if tt > 0 else (p - 1, 3)
                    if prev in out_dma_v:
                        wait("vector", cOUT, out_dma_v[prev])
                    for bi, bank in ((0, pb[0]), (1, pb[1])):
                        op("vector", lambda e, bi=bi, bank=bank: e.tensor_copy(
                            out=out_tm[:, bi * 512:(bi + 1) * 512],
                            in_=bank[:, 0:512]), eDVE, 1)
                    v_evT = eDVE.v
                    bank_release(0, eDVE, v_evT)
                    bank_release(1, eDVE, v_evT)
                    wait("gpsimd", eDVE, v_evT)
                    out_dma_v[(p, tt)] = op("gpsimd", lambda e, tt=tt, tok0=tok0:
                                            e.dma_start(
                                                out=d_out[tok0 + tt * 128:
                                                          tok0 + (tt + 1) * 128, :],
                                                in_=out_tm[:]), cOUT, 16)

            expert_blocks = []
            for ee in range(NE):
                expert_blocks.append(lambda ee=ee: b_w1(ee, 0))
                expert_blocks.append(lambda ee=ee: b_w1(ee, 1))
                expert_blocks.append(lambda ee=ee: b_w2(ee, 0))
                expert_blocks.append(lambda ee=ee: b_w2(ee, 1))
            return b_gate, expert_blocks, b_ln3out

        for c in (0, 1):
            if c < nchunk:
                stages, x2b_stage = make_front_stages(c)
                for stg in stages:
                    stg()
                x2b_stage()
        for p in range(npass):
            b_gate, expert_blocks, b_ln3out = make_moe_pass(p)
            fstages = []
            x2b_stages = []
            for cc in (2 * p + 2, 2 * p + 3):
                if cc < nchunk:
                    stages, x2b_stage = make_front_stages(cc)
                    fstages += stages
                    x2b_stages.append(x2b_stage)
            b_gate()
            nb = len(expert_blocks)
            nf = len(fstages)
            fi = 0
            for bi, blk_fn in enumerate(expert_blocks):
                blk_fn()
                want = ((bi + 1) * nf) // nb
                while fi < want:
                    fstages[fi]()
                    fi += 1
            while fi < nf:
                fstages[fi]()
                fi += 1
            b_ln3out()
            for stg in x2b_stages:
                stg()

        wait("gpsimd", cOUT, cOUT.v)
        if dbg:
            wait("gpsimd", cDBG, cDBG.v)

        counts = {k: len(v) for k, v in prog.items()}

        def runner(name):
            def body(eng):
                for fn in prog[name]:
                    fn(eng)
            return body

        blk = st.enter_context(nc.Block())
        blk.sync(runner("sync"))
        blk.tensor(runner("tensor"))
        blk.vector(runner("vector"))
        blk.scalar(runner("scalar"))
        blk.gpsimd(runner("gpsimd"))

    nc._instr_counts = counts
    return nc


def host_inputs(x_core, w):
    def fm(v):
        return np.ascontiguousarray(np.asarray(v, np.float32).reshape(NJ, 128).T)

    sel16 = np.zeros((NS, NS * 128), np.float32)
    for s in range(NS):
        sel16[s, s * 128:(s + 1) * 128] = 1.0
    sel8 = np.zeros((NE, NE * 128), np.float32)
    for s in range(NE):
        sel8[s, s * 128:(s + 1) * 128] = 1.0
    b1 = np.asarray(w["L0_ffn_b1"], np.float32)
    b1_fm = np.ascontiguousarray(
        b1.reshape(NE, 32, 128).transpose(2, 0, 1).reshape(128, NE * 32))
    conv_w = np.asarray(w["L0_attn_conv_w"], np.float32)[:, 0, :]
    conv_w_fm = np.ascontiguousarray(
        conv_w.reshape(NJ, 128, KCONV).transpose(1, 0, 2).reshape(128, NJ * KCONV))
    f32 = lambda k: np.ascontiguousarray(np.asarray(w[k], np.float32))
    return dict(
        x=np.ascontiguousarray(np.asarray(x_core, np.float32)),
        ln_g_fm=fm(w["L0_attn_ln_g"]), ln_b_fm=fm(w["L0_attn_ln_b"]),
        W_in=f32("L0_attn_W_in"), conv_w_fm=conv_w_fm,
        conv_b_fm=fm(w["L0_attn_conv_b"]),
        dt_w=f32("L0_attn_dt_w"), dt_b_fm=fm(w["L0_attn_dt_b"]),
        B_w=f32("L0_attn_B_w"), C_w=f32("L0_attn_C_w"),
        log_A_fm=fm(w["L0_attn_log_A"]), D_fm=fm(w["L0_attn_D_param"]),
        W_out=f32("L0_attn_W_out"), Wg=f32("L0_ffn_Wg"),
        bg_row=np.ascontiguousarray(np.asarray(w["L0_ffn_bg"], np.float32)
                                    .reshape(1, NE)),
        W1=f32("L0_ffn_W1"), b1_fm=b1_fm, W2=f32("L0_ffn_W2"), b2=f32("L0_ffn_b2"),
        ident=np.eye(128, dtype=np.float32),
        ones_col=np.ones((128, 1), np.float32),
        ones_row=np.ones((1, 128), np.float32),
        sel16=sel16, sel8=sel8,
    )


def kernel(**inputs):
    x = np.asarray(inputs["L0_attn_x"])
    B, T, _ = x.shape
    nc = build_nc(T, detect_races=False)
    in_maps = [host_inputs(x[i], inputs) for i in range(B)]
    res = run_bass_kernel_spmd(nc, in_maps, core_ids=list(range(B)))
    return np.stack([r["out"] for r in res.results], axis=0)


# version 13
# speedup vs baseline: 1.3465x; 1.1936x over previous
import contextlib

import numpy as np

import concourse.bass as bass
import concourse.mybir as mybir
from concourse.bass_utils import run_bass_kernel_spmd

F32 = mybir.dt.float32
BF16 = mybir.dt.bfloat16
AF = mybir.ActivationFunctionType
ALU = mybir.AluOpType
AX = mybir.AxisListType

D = 1024
NS = 16
KCONV = 4
DH = 4096
NE = 8
NJ = 8
LC = 256
SC = 512
SG = 4
LN_EPS = 1e-5

ENGINES = ("sync", "tensor", "vector", "scalar", "gpsimd")


class Ctr:
    def __init__(self, st, nc, name):
        self.sem = st.enter_context(nc.semaphore(name))
        self.v = 0


def build_nc(T, detect_races=True, dbg=False):
    nchunk = T // LC
    npass = T // SC
    nc = bass.Bass("TRN2", target_bir_lowering=False,
                   detect_race_conditions=detect_races)

    dx = nc.dram_tensor("x", [T, D], F32, kind="ExternalInput").ap()
    d_ln_g = nc.dram_tensor("ln_g_fm", [128, NJ], F32, kind="ExternalInput").ap()
    d_ln_b = nc.dram_tensor("ln_b_fm", [128, NJ], F32, kind="ExternalInput").ap()
    d_Win = nc.dram_tensor("W_in", [D, 2 * D], F32, kind="ExternalInput").ap()
    d_convw = nc.dram_tensor("conv_w_fm", [128, NJ * KCONV], F32, kind="ExternalInput").ap()
    d_convb = nc.dram_tensor("conv_b_fm", [128, NJ], F32, kind="ExternalInput").ap()
    d_dtw = nc.dram_tensor("dt_w", [D, D], F32, kind="ExternalInput").ap()
    d_dtb = nc.dram_tensor("dt_b_fm", [128, NJ], F32, kind="ExternalInput").ap()
    d_Bw = nc.dram_tensor("B_w", [D, NS], F32, kind="ExternalInput").ap()
    d_Cw = nc.dram_tensor("C_w", [D, NS], F32, kind="ExternalInput").ap()
    d_logA = nc.dram_tensor("log_A_fm", [128, NJ], F32, kind="ExternalInput").ap()
    d_Dp = nc.dram_tensor("D_fm", [128, NJ], F32, kind="ExternalInput").ap()
    d_Wout = nc.dram_tensor("W_out", [D, D], F32, kind="ExternalInput").ap()
    d_Wg = nc.dram_tensor("Wg", [D, NE], F32, kind="ExternalInput").ap()
    d_bg = nc.dram_tensor("bg_row", [1, NE], F32, kind="ExternalInput").ap()
    d_W1 = nc.dram_tensor("W1", [NE, D, DH], F32, kind="ExternalInput").ap()
    d_b1 = nc.dram_tensor("b1_fm", [128, NE * 32], F32, kind="ExternalInput").ap()
    d_W2 = nc.dram_tensor("W2", [NE, DH, D], F32, kind="ExternalInput").ap()
    d_b2 = nc.dram_tensor("b2", [NE, D], F32, kind="ExternalInput").ap()
    d_ident = nc.dram_tensor("ident", [128, 128], F32, kind="ExternalInput").ap()
    d_ones_col = nc.dram_tensor("ones_col", [128, 1], F32, kind="ExternalInput").ap()
    d_ones_row = nc.dram_tensor("ones_row", [1, 128], F32, kind="ExternalInput").ap()
    d_sel16 = nc.dram_tensor("sel16", [NS, NS * 128], F32, kind="ExternalInput").ap()
    d_sel8 = nc.dram_tensor("sel8", [NE, NE * 128], F32, kind="ExternalInput").ap()

    d_out = nc.dram_tensor("out", [T, D], F32, kind="ExternalOutput").ap()

    s_Win = nc.dram_tensor("s_Win", [16, 128, 8 * 128], BF16, kind="Internal").ap()
    s_dtw = nc.dram_tensor("s_dtw", [8, 128, 8 * 128], BF16, kind="Internal").ap()
    s_Wout = nc.dram_tensor("s_Wout", [8, 128, 8 * 128], BF16, kind="Internal").ap()
    s_W1 = nc.dram_tensor("s_W1", [NE, 16, 128, 8 * 256], BF16, kind="Internal").ap()
    s_W2 = nc.dram_tensor("s_W2", [NE, 8, 128, 32 * 128], BF16, kind="Internal").ap()

    st = contextlib.ExitStack()
    with st:
        def sb(name, shape, dt):
            return st.enter_context(nc.sbuf_tensor(name, shape, dt))

        ident_f = sb("ident_f", [128, 128], F32)
        ones_col_f = sb("ones_col_f", [128, 1], F32)
        ones_col_b = sb("ones_col_b", [128, 1], BF16)
        ones_row_f = sb("ones_row_f", [1, 128], F32)
        sel16_b = sb("sel16_b", [NS, NS * 128], BF16)
        sel8_b = sb("sel8_b", [NE, NE * 128], BF16)
        ln_g = sb("ln_g", [128, NJ], F32)
        ln_b = sb("ln_b", [128, NJ], F32)
        conv_w = sb("conv_w", [128, NJ * KCONV], F32)
        conv_b = sb("conv_b", [128, NJ], F32)
        dt_b = sb("dt_b", [128, NJ], F32)
        A_neg = sb("A_neg", [128, NJ], F32)
        D_fm = sb("D_fm_s", [128, NJ], F32)
        b1_fm = sb("b1_fm_s", [128, NE * 32], F32)
        b2_b = sb("b2_b", [NE, D], BF16)
        bg_row_s = sb("bg_row_s", [1, NE], F32)
        bg_bcast = sb("bg_bcast", [128, NE], F32)
        Wg_sb = sb("Wg_sb", [128, 8 * NE], F32)
        Bw_b = sb("Bw_b", [128, 8 * NS], BF16)
        Cw_b = sb("Cw_b", [128, 8 * NS], BF16)
        h_state = sb("h_state", [128, NJ * NS], BF16)
        hist = sb("hist", [128, NJ * 3], F32)
        pre = sb("pre", [128, LC + 3], F32)
        Btm = sb("Btm", [128, 2 * NS], F32)
        Ctm = sb("Ctm", [128, 2 * NS], F32)
        Bfm_b = sb("Bfm_b", [NS, LC], BF16)
        Cfm_b = sb("Cfm_b", [NS, LC], BF16)
        mu_row = sb("mu_row", [1, SC], F32)
        var_row = sb("var_row", [1, SC], F32)
        rstd_row = sb("rstd_row", [1, SC], F32)
        tmp_row = sb("tmp_row", [1, SC], F32)
        junk = sb("junk", [128, 256], F32)

        fw = sb("fw", [128, 3 * 1024], BF16)
        w1r = sb("w1r", [128, 3 * 2048], BF16)
        w2r = sb("w2r", [128, 3 * 2048], BF16)

        x_tm = sb("x_tm", [128, 2 * 1024], F32)
        out_tm = sb("out_tm", [128, 1024], F32)
        x_fm = sb("x_fm", [128, NJ * LC], F32)
        xn_b = sb("xn_b", [128, NJ * LC], BF16)
        xsq_b = sb("xsq_b", [128, NJ * LC], BF16)
        z_silu = sb("z_silu", [128, NJ * LC], BF16)
        x_ssm = sb("x_ssm", [128, NJ * LC], F32)
        x_ssm_b = sb("x_ssm_b", [128, NJ * LC], BF16)
        u_buf = sb("u_buf", [128, NJ * LC], BF16)
        B_sb = sb("B_sb", [128, SG * LC], BF16)
        C_sb = sb("C_sb", [128, SG * LC], BF16)
        hcb = sb("hcb", [128, SG * LC], BF16)
        d1b = sb("d1b", [128, LC], BF16)
        decay = sb("decay", [128, NJ * LC], F32)
        dt_j = sb("dt_j", [128, LC], F32)
        sp1 = sb("sp1", [128, LC], F32)
        sp2 = sb("sp2", [128, LC], F32)
        h_traj = sb("h_traj", [128, SG * LC], BF16)
        hc = sb("hc", [128, 2 * LC], F32)
        y_buf = sb("y_buf", [128, NJ * LC], F32)

        x2 = sb("x2", [128, NJ * SC], F32)
        x2b = sb("x2b", [128, NJ * SC], BF16)
        h1 = sb("h1", [128, 32 * SC], BF16)
        o_acc = sb("o_acc", [128, NJ * SC], F32)
        tmp_o = sb("tmp_o", [128, SC], F32)
        mask_b = sb("mask_b", [128, SC], F32)
        gate_tm = sb("gate_tm", [128, 4 * NE], F32)
        probs = sb("probs", [128, 4 * NE], F32)
        mask_tm = sb("mask_tm", [128, 4 * NE], F32)
        gsc = sb("gsc", [128, 8], F32)
        w_tm = sb("w_tm", [128, 4], F32)
        maskw_fm_b = sb("maskw_fm_b", [NE, SC], BF16)

        pb = [st.enter_context(nc.psum_tensor(f"pb{i}", [128, 512], F32))
              for i in range(8)]

        cSM = Ctr(st, nc, "cSM")
        cXs = [Ctr(st, nc, f"cX{i}") for i in range(2)]
        cFWs = [Ctr(st, nc, f"cFW{i}") for i in range(3)]
        cW1s = [Ctr(st, nc, f"cW1{i}") for i in range(3)]
        cW2s = [Ctr(st, nc, f"cW2{i}") for i in range(3)]
        cCVsm = Ctr(st, nc, "cCVsm")
        cCVF = Ctr(st, nc, "cCVF")
        cCVdt = Ctr(st, nc, "cCVdt")
        cCVwo = Ctr(st, nc, "cCVwo")
        cCVe = [Ctr(st, nc, f"cCVe{i}") for i in range(NE)]
        cOUT = Ctr(st, nc, "cOUT")
        cDBG = Ctr(st, nc, "cDBG")
        ePE = Ctr(st, nc, "ePE")
        eDVE = Ctr(st, nc, "eDVE")
        eACT = Ctr(st, nc, "eACT")

        prog = {e: [] for e in ENGINES}

        def em(eng, fn):
            prog[eng].append(fn)

        def wait(eng, ctr, val=None):
            if isinstance(ctr, tuple):
                sem, v = ctr
            else:
                sem, v = ctr.sem, (ctr.v if val is None else val)
            if v > 0:
                em(eng, lambda e, s=sem, v=v: e.wait_ge(s, v))

        def op(eng, fn, ctr=None, n=1):
            if ctr is None or n is None:
                em(eng, fn)
                return None
            ctr.v += n
            em(eng, lambda e, fn=fn, s=ctr.sem, n=n: fn(e).then_inc(s, n))
            return ctr.v

        def fence_dve():
            op("vector", lambda e: e.tensor_copy(out=junk[:], in_=junk[:]), eDVE)

        def fence_act():
            op("scalar", lambda e: e.activation(out=junk[:], in_=junk[:],
                                                func=AF.Identity), eACT)

        bank_last = {i: None for i in range(8)}

        def bank_acquire(eng, i):
            if bank_last[i] is not None:
                wait(eng, bank_last[i])

        def bank_release(i, ctr, val):
            bank_last[i] = (ctr.sem, val)

        dbg_tensors = {}

        def dbg_dump(name, src_ap, shape, dt, dep_ctr, dep_val,
                     stall=("vector", "scalar", "tensor")):
            if not dbg:
                return
            dtn = nc.dram_tensor("dbg_" + name, shape, dt, kind="ExternalOutput").ap()
            dbg_tensors[name] = dtn
            wait("gpsimd", dep_ctr, dep_val)
            v = op("gpsimd", lambda e, dtn=dtn, src_ap=src_ap: e.dma_start(
                out=dtn, in_=src_ap), cDBG, 16)
            for en in stall:
                wait(en, cDBG, v)

        for fn in [
            lambda e: e.dma_start(out=ident_f[:], in_=d_ident),
            lambda e: e.dma_start(out=ones_col_f[:], in_=d_ones_col),
            lambda e: e.dma_start(out=ones_row_f[:], in_=d_ones_row),
            lambda e: e.dma_start(out=ln_g[:], in_=d_ln_g),
            lambda e: e.dma_start(out=ln_b[:], in_=d_ln_b),
            lambda e: e.dma_start(out=conv_w[:], in_=d_convw),
            lambda e: e.dma_start(out=conv_b[:], in_=d_convb),
            lambda e: e.dma_start(out=dt_b[:], in_=d_dtb),
            lambda e: e.dma_start(out=A_neg[:], in_=d_logA),
            lambda e: e.dma_start(out=D_fm[:], in_=d_Dp),
            lambda e: e.dma_start(out=b1_fm[:], in_=d_b1),
            lambda e: e.dma_start(out=bg_row_s[:], in_=d_bg),
        ]:
            op("sync", fn, cSM, 16)
        op("sync", lambda e: e.dma_start(
            out=Wg_sb[:].rearrange("p (ks e) -> p ks e", e=NE),
            in_=d_Wg.rearrange("(ks p) e -> p ks e", p=128),
            allow_slow_non_contiguous=True), cSM, 16)
        v_small = cSM.v

        op("gpsimd", lambda e: e.dma_start(out=sel16_b[:], in_=d_sel16), cCVsm, 16)
        op("gpsimd", lambda e: e.dma_start(out=sel8_b[:], in_=d_sel8), cCVsm, 16)
        op("gpsimd", lambda e: e.dma_start(out=b2_b[:], in_=d_b2), cCVsm, 16)
        op("gpsimd", lambda e: e.dma_start(
            out=Bw_b[:].rearrange("p (ks s) -> p ks s", s=NS),
            in_=d_Bw.rearrange("(ks p) s -> p ks s", p=128),
            allow_slow_non_contiguous=True), cCVsm, 16)
        op("gpsimd", lambda e: e.dma_start(
            out=Cw_b[:].rearrange("p (ks s) -> p ks s", s=NS),
            in_=d_Cw.rearrange("(ks p) s -> p ks s", p=128),
            allow_slow_non_contiguous=True), cCVsm, 16)
        v_cast_small = cCVsm.v

        wait("scalar", cSM, v_small)
        v_exp = op("scalar", lambda e: e.activation(out=A_neg[:], in_=A_neg[:],
                                                    func=AF.Exp), eACT)
        wait("vector", eACT, v_exp)
        op("vector", lambda e: e.tensor_copy(out=ones_col_b[:], in_=ones_col_f[:]), eDVE)
        op("vector", lambda e: e.memset(h_state[:], 0.0), eDVE)
        op("vector", lambda e: e.memset(hist[:], 0.0), eDVE)
        op("vector", lambda e: e.memset(junk[:], 0.0), eDVE)
        op("vector", lambda e: e.tensor_scalar(out=A_neg[:], in0=A_neg[:],
                                               scalar1=-1.0, scalar2=None,
                                               op0=ALU.mult), eDVE)
        v_dset = eDVE.v
        wait("tensor", cSM, v_small)
        bank_acquire("tensor", 4)
        v_bgmm = op("tensor", lambda e: e.matmul(pb[4][:, 0:NE], ones_row_f[0:1, :],
                                                 bg_row_s[0:1, :],
                                                 start=True, stop=True), ePE, 1)
        wait("vector", ePE, v_bgmm)
        fence_dve()
        op("vector", lambda e: e.tensor_copy(out=bg_bcast[:], in_=pb[4][:, 0:NE]), eDVE)
        bank_release(4, eDVE, eDVE.v)

        for mt in range(16):
            src = d_Win.rearrange("(ks p) (mt m) -> mt p ks m", p=128, m=128)[mt]
            op("gpsimd", lambda e, src=src, mt=mt: e.dma_start(
                out=s_Win[mt].rearrange("p (ks m) -> p ks m", m=128), in_=src,
                allow_slow_non_contiguous=True), cCVF, 16)
        for mt in range(8):
            src = d_dtw.rearrange("(ks p) (mt m) -> mt p ks m", p=128, m=128)[mt]
            op("gpsimd", lambda e, src=src, mt=mt: e.dma_start(
                out=s_dtw[mt].rearrange("p (ks m) -> p ks m", m=128), in_=src,
                allow_slow_non_contiguous=True), cCVdt, 16)
        for mt in range(8):
            src = d_Wout.rearrange("(ks p) (mt m) -> mt p ks m", p=128, m=128)[mt]
            op("gpsimd", lambda e, src=src, mt=mt: e.dma_start(
                out=s_Wout[mt].rearrange("p (ks m) -> p ks m", m=128), in_=src,
                allow_slow_non_contiguous=True), cCVwo, 16)
        v_cv_front = cCVF.v
        v_cv_dt = cCVdt.v
        v_cv_wo = cCVwo.v
        v_cv_e = []
        for ee in range(NE):
            for hu in range(16):
                src = d_W1[ee].rearrange("(ks p) (hu h) -> hu p ks h", p=128, h=256)[hu]
                op("gpsimd", lambda e, src=src, ee=ee, hu=hu: e.dma_start(
                    out=s_W1[ee, hu].rearrange("p (ks h) -> p ks h", h=256), in_=src,
                    allow_slow_non_contiguous=True), cCVe[ee], 16)
            for dt in range(8):
                for hq in range(4):
                    src = d_W2[ee].rearrange(
                        "(hq hs p) (dt d) -> dt hq p hs d", hq=4, hs=8, p=128,
                        d=128)[dt, hq]
                    dst = s_W2[ee, dt].rearrange("p (hq hs d) -> hq p hs d",
                                                 hq=4, hs=8)[hq]
                    op("gpsimd", lambda e, src=src, dst=dst, ee=ee: e.dma_start(
                        out=dst, in_=src, allow_slow_non_contiguous=True), cCVe[ee], 16)
            v_cv_e.append((cCVe[ee].sem, cCVe[ee].v))

        xload_v = {}
        xload_consumed = {}

        def load_x_tile(gt):
            if gt - 2 in xload_consumed:
                wait("sync", ePE, xload_consumed[gt - 2])
            sl = (gt % 2) * 1024
            c_ = cXs[gt % 2]
            v = op("sync", lambda e, gt=gt, sl=sl: e.dma_start(
                out=x_tm[:, sl:sl + 1024], in_=dx[gt * 128:(gt + 1) * 128, :]), c_, 16)
            xload_v[gt] = (c_.sem, v)

        fw_state = {"count": 0, "loaded": {}, "consumed": {}}

        def stream_front_unit(scr_ap):
            u = fw_state["count"]
            if u >= 3:
                wait("sync", ePE, fw_state["consumed"][u - 3])
            sl = (u % 3) * 1024
            c_ = cFWs[u % 3]
            v = op("sync", lambda e, scr=scr_ap, sl=sl: e.dma_start(
                out=fw[:, sl:sl + 1024], in_=scr), c_, 16)
            fw_state["loaded"][u] = (c_.sem, v)
            fw_state["count"] = u + 1
            return u, sl

        chunk_S = {}
        pass_S = {}
        scan_state = {"v_mul_last": None}

        def fm_ln_stats_apply(src, ncols, writer, v_src_dve):
            op("vector", lambda e: e.tensor_copy(out=xsq_b[:, 0:NJ * ncols],
                                                 in_=src[:, 0:NJ * ncols]), eDVE)
            v_c1 = eDVE.v
            wait("scalar", eDVE, v_src_dve)
            op("scalar", lambda e: e.activation(out=z_silu[:, 0:NJ * ncols],
                                                in_=src[:, 0:NJ * ncols],
                                                func=AF.Square), eACT)
            v_c2 = eACT.v
            wait("tensor", eDVE, v_c1)
            wait("tensor", eACT, v_c2)
            bank_acquire("tensor", 4)
            bank_acquire("tensor", 5)
            for j in range(NJ):
                op("tensor", lambda e, j=j: e.matmul(
                    pb[4][0:1, 0:ncols], ones_col_b[:],
                    xsq_b[:, j * ncols:(j + 1) * ncols],
                    start=(j == 0), stop=(j == NJ - 1)), ePE, 1 if j == NJ - 1 else None)
            for j in range(NJ):
                op("tensor", lambda e, j=j: e.matmul(
                    pb[5][0:1, 0:ncols], ones_col_b[:],
                    z_silu[:, j * ncols:(j + 1) * ncols],
                    start=(j == 0), stop=(j == NJ - 1)), ePE, 1 if j == NJ - 1 else None)
            v_stat = ePE.v
            wait("vector", ePE, v_stat)
            fence_dve()
            op("vector", lambda e: e.tensor_scalar(out=mu_row[0:1, 0:ncols],
                                                   in0=pb[4][0:1, 0:ncols],
                                                   scalar1=1.0 / D, scalar2=None,
                                                   op0=ALU.mult), eDVE)
            op("vector", lambda e: e.tensor_mul(out=tmp_row[0:1, 0:ncols],
                                                in0=mu_row[0:1, 0:ncols],
                                                in1=mu_row[0:1, 0:ncols]), eDVE)
            op("vector", lambda e: e.scalar_tensor_tensor(
                out=var_row[0:1, 0:ncols], in0=pb[5][0:1, 0:ncols], scalar=1.0 / D,
                in1=tmp_row[0:1, 0:ncols], op0=ALU.mult, op1=ALU.subtract), eDVE)
            op("vector", lambda e: e.tensor_scalar(
                out=var_row[0:1, 0:ncols], in0=var_row[0:1, 0:ncols], scalar1=LN_EPS,
                scalar2=None, op0=ALU.add), eDVE)
            fence_dve()
            v_var = eDVE.v
            wait("scalar", eDVE, v_var)
            op("scalar", lambda e: e.activation(out=tmp_row[0:1, 0:ncols],
                                                in_=var_row[0:1, 0:ncols],
                                                func=AF.Ln), eACT)
            op("scalar", lambda e: e.activation(out=rstd_row[0:1, 0:ncols],
                                                in_=tmp_row[0:1, 0:ncols],
                                                func=AF.Exp, scale=-0.5), eACT)
            fence_act()
            v_rstd = eACT.v
            wait("tensor", eACT, v_rstd)
            op("tensor", lambda e: e.matmul(pb[4][:, 0:ncols], ones_row_f[0:1, :],
                                            mu_row[0:1, 0:ncols],
                                            start=True, stop=True), ePE)
            op("tensor", lambda e: e.matmul(pb[5][:, 0:ncols], ones_row_f[0:1, :],
                                            rstd_row[0:1, 0:ncols],
                                            start=True, stop=True), ePE, 1)
            v_bcst = ePE.v
            wait("vector", ePE, v_bcst)
            for j in range(NJ):
                writer(j)
            bank_release(4, eDVE, eDVE.v)
            bank_release(5, eDVE, eDVE.v)

        def make_front_stages(c):
            cs = chunk_S.setdefault(c, {})
            half = c % 2
            p_owner = c // 2
            stages = []

            def s_load():
                for tt in range(2):
                    load_x_tile(2 * c + tt)
            stages.append(s_load)

            def s_transpose():
                for tt in range(2):
                    gt = 2 * c + tt
                    sl = (gt % 2) * 1024
                    wait("tensor", xload_v[gt])
                    if c == 0 and tt == 0:
                        wait("tensor", cSM, v_small)
                    bank_acquire("tensor", 4)
                    bank_acquire("tensor", 5)
                    for j in range(NJ):
                        bank = pb[4] if j < 4 else pb[5]
                        colo = (j % 4) * 128
                        op("tensor", lambda e, sl=sl, j=j, bank=bank, colo=colo:
                           e.transpose(bank[:, colo:colo + 128],
                                       x_tm[:, sl + j * 128: sl + (j + 1) * 128],
                                       ident_f[:]), ePE, 1 if j == NJ - 1 else None)
                    v_tr = ePE.v
                    xload_consumed[gt] = v_tr
                    wait("vector", ePE, v_tr)
                    for bi, bank in ((0, pb[4]), (1, pb[5])):
                        dst = x_fm[:].rearrange("p (j t) -> p j t", j=NJ)[
                            :, bi * 4:(bi + 1) * 4, tt * 128:(tt + 1) * 128]
                        srcv = bank[:].rearrange("p (j t) -> p j t", j=4)
                        op("vector", lambda e, dst=dst, srcv=srcv: e.tensor_copy(
                            out=dst, in_=srcv), eDVE, 1)
                    bank_release(4, eDVE, eDVE.v)
                    bank_release(5, eDVE, eDVE.v)
                cs["v_xfm"] = eDVE.v
            stages.append(s_transpose)

            def s_ln1():
                if c == 0:
                    wait("vector", eDVE, v_dset)

                def writer(j):
                    op("vector", lambda e, j=j: e.tensor_sub(
                        out=hc[:, 0:LC], in0=x_fm[:, j * LC:(j + 1) * LC],
                        in1=pb[4][:, 0:LC]), eDVE)
                    op("vector", lambda e: e.tensor_mul(
                        out=hc[:, LC:2 * LC], in0=hc[:, 0:LC], in1=pb[5][:, 0:LC]),
                        eDVE)
                    op("vector", lambda e, j=j: e.tensor_scalar(
                        out=xn_b[:, j * LC:(j + 1) * LC], in0=hc[:, LC:2 * LC],
                        scalar1=ln_g[:, j:j + 1], scalar2=ln_b[:, j:j + 1],
                        op0=ALU.mult, op1=ALU.add), eDVE)
                fm_ln_stats_apply(x_fm, LC, writer, cs["v_xfm"])
                cs["v_xn"] = eDVE.v
            stages.append(s_ln1)

            def s_win(mt0, mt1):
                for mt in range(mt0, mt1):
                    if c == 0 and mt == 0:
                        wait("sync", cCVF, v_cv_front)
                    u, sl = stream_front_unit(s_Win[mt])
                    wait("tensor", fw_state["loaded"][u])
                    if mt == mt0:
                        wait("tensor", eDVE, cs["v_xn"])
                    bank_i = 4 + (mt % 4)
                    bank = pb[bank_i]
                    bank_acquire("tensor", bank_i)
                    for ks in range(8):
                        op("tensor", lambda e, sl=sl, ks=ks, bank=bank: e.matmul(
                            bank[:, 0:LC], fw[:, sl + ks * 128: sl + (ks + 1) * 128],
                            xn_b[:, ks * LC:(ks + 1) * LC],
                            start=(ks == 0), stop=(ks == 7)), ePE, 1 if ks == 7 else None)
                    fw_state["consumed"][u] = ePE.v
                    v_mm = ePE.v
                    if mt < 8:
                        j = mt
                        wait("vector", ePE, v_mm)
                        op("vector", lambda e, j=j: e.tensor_copy(
                            out=pre[:, 0:3], in_=hist[:, j * 3:(j + 1) * 3]), eDVE)
                        op("vector", lambda e, bank=bank: e.tensor_copy(
                            out=pre[:, 3:3 + LC], in_=bank[:, 0:LC]), eDVE)
                        bank_release(bank_i, eDVE, eDVE.v)
                        if cs.get("v_convsilu_prev") is not None:
                            wait("vector", eACT, cs["v_convsilu_prev"])
                        op("vector", lambda e, j=j: e.tensor_scalar(
                            out=sp1[:], in0=pre[:, 0:LC],
                            scalar1=conv_w[:, j * KCONV:j * KCONV + 1], scalar2=None,
                            op0=ALU.mult), eDVE)
                        for k in range(1, KCONV):
                            op("vector", lambda e, k=k, j=j: e.scalar_tensor_tensor(
                                out=sp1[:], in0=pre[:, k:k + LC],
                                scalar=conv_w[:, j * KCONV + k:j * KCONV + k + 1],
                                in1=sp1[:], op0=ALU.mult, op1=ALU.add), eDVE)
                        op("vector", lambda e, j=j: e.tensor_copy(
                            out=hist[:, j * 3:(j + 1) * 3], in_=pre[:, LC:LC + 3]),
                            eDVE)
                        v_conv = eDVE.v
                        wait("scalar", eDVE, v_conv)
                        op("scalar", lambda e, j=j: e.activation(
                            out=x_ssm[:, j * LC:(j + 1) * LC], in_=sp1[:],
                            func=AF.Silu, bias=conv_b[:, j:j + 1]), eACT)
                        cs["v_convsilu_prev"] = eACT.v
                    else:
                        j = mt - 8
                        wait("scalar", ePE, v_mm)
                        op("scalar", lambda e, j=j, bank=bank: e.activation(
                            out=z_silu[:, j * LC:(j + 1) * LC], in_=bank[:, 0:LC],
                            func=AF.Silu), eACT)
                        bank_release(bank_i, eACT, eACT.v)
                if mt1 == 16:
                    v_front_act = eACT.v
                    wait("vector", eACT, v_front_act)
                    op("vector", lambda e: e.tensor_copy(out=x_ssm_b[:], in_=x_ssm[:]),
                       eDVE)
                    cs["v_xssm_b"] = eDVE.v
            stages.append(lambda: s_win(0, 8))
            stages.append(lambda: s_win(8, 16))

            def s_dt(j0, j1):
                for j in range(j0, j1):
                    if c == 0 and j == 0:
                        wait("sync", cCVdt, v_cv_dt)
                    u, sl = stream_front_unit(s_dtw[j])
                    wait("tensor", fw_state["loaded"][u])
                    if j == j0:
                        wait("tensor", eDVE, cs["v_xssm_b"])
                    v_sp_war = cs.get("v_du_prev")
                    bank_i = 4 + (j % 4)
                    bank = pb[bank_i]
                    bank_acquire("tensor", bank_i)
                    for ks in range(8):
                        op("tensor", lambda e, sl=sl, ks=ks, bank=bank: e.matmul(
                            bank[:, 0:LC], fw[:, sl + ks * 128: sl + (ks + 1) * 128],
                            x_ssm_b[:, ks * LC:(ks + 1) * LC],
                            start=(ks == 0), stop=(ks == 7)), ePE, 1 if ks == 7 else None)
                    fw_state["consumed"][u] = ePE.v
                    v_mm = ePE.v
                    wait("scalar", ePE, v_mm)
                    if v_sp_war is not None:
                        wait("scalar", eDVE, v_sp_war)
                    op("scalar", lambda e, j=j, bank=bank: e.activation(
                        out=sp1[:], in_=bank[:, 0:LC], func=AF.Abs,
                        bias=dt_b[:, j:j + 1]), eACT)
                    op("scalar", lambda e: e.activation(out=sp1[:], in_=sp1[:],
                                                        func=AF.Exp, scale=-1.0), eACT)
                    op("scalar", lambda e: e.activation(out=sp1[:], in_=sp1[:],
                                                        func=AF.Ln, bias=1.0), eACT)
                    op("scalar", lambda e, j=j, bank=bank: e.activation(
                        out=sp2[:], in_=bank[:, 0:LC], func=AF.Relu,
                        bias=dt_b[:, j:j + 1]), eACT)
                    v_sp = eACT.v
                    bank_release(bank_i, eACT, v_sp)
                    wait("vector", eACT, v_sp)
                    op("vector", lambda e: e.tensor_add(out=dt_j[:], in0=sp1[:],
                                                        in1=sp2[:]), eDVE)
                    op("vector", lambda e, j=j: e.tensor_mul(
                        out=u_buf[:, j * LC:(j + 1) * LC], in0=dt_j[:],
                        in1=x_ssm[:, j * LC:(j + 1) * LC]), eDVE)
                    op("vector", lambda e, j=j: e.tensor_scalar(
                        out=sp2[:], in0=dt_j[:], scalar1=A_neg[:, j:j + 1],
                        scalar2=None, op0=ALU.mult), eDVE)
                    v_du = eDVE.v
                    cs["v_du_prev"] = v_du
                    wait("scalar", eDVE, v_du)
                    op("scalar", lambda e, j=j: e.activation(
                        out=decay[:, j * LC:(j + 1) * LC], in_=sp2[:], func=AF.Exp),
                        eACT)
                if j1 == NJ:
                    cs["v_decay"] = eACT.v
            stages.append(lambda: s_dt(0, 4))
            stages.append(lambda: s_dt(4, 8))

            def s_bc():
                wait("tensor", eDVE, cs["v_xssm_b"])
                if c == 0:
                    wait("tensor", cCVsm, v_cast_small)
                bank_acquire("tensor", 4)
                for w_i, wsb in ((0, Bw_b), (1, Cw_b)):
                    for tt in range(2):
                        for ks in range(8):
                            op("tensor", lambda e, w_i=w_i, wsb=wsb, tt=tt, ks=ks:
                               e.matmul(
                                   pb[4][:, (w_i * 2 + tt) * NS:
                                         (w_i * 2 + tt + 1) * NS],
                                   x_ssm_b[:, ks * LC + tt * 128:
                                           ks * LC + tt * 128 + 128],
                                   wsb[:, ks * NS:(ks + 1) * NS],
                                   start=(ks == 0), stop=(ks == 7)),
                               ePE, 1 if (w_i == 1 and tt == 1 and ks == 7) else None)
                v_bcmm = ePE.v
                wait("vector", ePE, v_bcmm)
                fence_dve()
                op("vector", lambda e: e.tensor_copy(out=Btm[:],
                                                     in_=pb[4][:, 0:2 * NS]), eDVE)
                op("vector", lambda e: e.tensor_copy(out=Ctm[:],
                                                     in_=pb[4][:, 2 * NS:4 * NS]), eDVE)
                bank_release(4, eDVE, eDVE.v)
                fence_dve()
                v_bctm = eDVE.v
                wait("tensor", eDVE, v_bctm)
                bank_acquire("tensor", 5)
                for w_i, wtm in ((0, Btm), (1, Ctm)):
                    for tt in range(2):
                        op("tensor", lambda e, w_i=w_i, wtm=wtm, tt=tt: e.transpose(
                            pb[5][0:NS, (w_i * 2 + tt) * 128:(w_i * 2 + tt + 1) * 128],
                            wtm[:, tt * NS:(tt + 1) * NS], ident_f[:]),
                            ePE, 1 if (w_i == 1 and tt == 1) else None)
                v_bcT = ePE.v
                wait("vector", ePE, v_bcT)
                fence_dve()
                op("vector", lambda e: e.tensor_copy(out=Bfm_b[:],
                                                     in_=pb[5][0:NS, 0:2 * 128]), eDVE)
                op("vector", lambda e: e.tensor_copy(out=Cfm_b[:],
                                                     in_=pb[5][0:NS, 2 * 128:4 * 128]),
                   eDVE)
                bank_release(5, eDVE, eDVE.v)
                fence_dve()
                cs["v_bcfm"] = eDVE.v
            stages.append(s_bc)

            def s_scan(g, jh):
                if jh == 0:
                    if g == 0:
                        wait("tensor", eDVE, cs["v_bcfm"])
                    for bank_i in (4, 5, 6, 7):
                        bank_acquire("tensor", bank_i)
                    for i in range(SG):
                        s_ = g * SG + i
                        bank = pb[6] if i < 2 else pb[7]
                        colo = (i % 2) * LC
                        op("tensor", lambda e, s_=s_, bank=bank, colo=colo: e.matmul(
                            bank[:, colo:colo + LC],
                            sel16_b[0:NS, s_ * 128:(s_ + 1) * 128],
                            Cfm_b[0:NS, :], start=True, stop=True),
                            ePE, 1 if i == SG - 1 else None)
                    for i in range(SG):
                        s_ = g * SG + i
                        bank = pb[4] if i < 2 else pb[5]
                        colo = (i % 2) * LC
                        op("tensor", lambda e, s_=s_, bank=bank, colo=colo: e.matmul(
                            bank[:, colo:colo + LC],
                            sel16_b[0:NS, s_ * 128:(s_ + 1) * 128],
                            Bfm_b[0:NS, :], start=True, stop=True),
                            ePE, 1 if i == SG - 1 else None)
                    cs["v_bg"] = ePE.v
                    wait("scalar", ePE, cs["v_bg"])
                    if scan_state["v_mul_last"] is not None:
                        wait("scalar", eDVE, scan_state["v_mul_last"])
                    op("scalar", lambda e: e.activation(
                        out=B_sb[:, 0:2 * LC], in_=pb[4][:, 0:2 * LC],
                        func=AF.Identity), eACT)
                    op("scalar", lambda e: e.activation(
                        out=B_sb[:, 2 * LC:4 * LC], in_=pb[5][:, 0:2 * LC],
                        func=AF.Identity), eACT)
                    op("scalar", lambda e: e.activation(
                        out=C_sb[:, 0:2 * LC], in_=pb[6][:, 0:2 * LC],
                        func=AF.Identity), eACT)
                    op("scalar", lambda e: e.activation(
                        out=C_sb[:, 2 * LC:4 * LC], in_=pb[7][:, 0:2 * LC],
                        func=AF.Identity), eACT)
                    cs["v_bcsb"] = eACT.v
                    for bank_i in (4, 5, 6, 7):
                        bank_release(bank_i, eACT, cs["v_bcsb"])
                    wait("vector", eACT, cs["v_bcsb"])
                for j in range(jh * 4, jh * 4 + 4):
                    for i in range(SG):
                        s_ = g * SG + i
                        op("vector", lambda e, j=j, i=i: e.tensor_mul(
                            out=d1b[:], in0=u_buf[:, j * LC:(j + 1) * LC],
                            in1=B_sb[:, i * LC:(i + 1) * LC]), eDVE)
                        op("vector", lambda e, j=j, i=i, s_=s_: e.tensor_tensor_scan(
                            out=h_traj[:, i * LC:(i + 1) * LC],
                            data0=decay[:, j * LC:(j + 1) * LC], data1=d1b[:],
                            initial=h_state[:, j * NS + s_:j * NS + s_ + 1],
                            op0=ALU.mult, op1=ALU.add), eDVE)
                    for i in range(SG):
                        op("vector", lambda e, i=i: e.tensor_mul(
                            out=hcb[:, i * LC:(i + 1) * LC],
                            in0=h_traj[:, i * LC:(i + 1) * LC],
                            in1=C_sb[:, i * LC:(i + 1) * LC]), eDVE)
                    scan_state["v_mul_last"] = eDVE.v
                    for i in range(SG):
                        op("vector", lambda e, j=j, g=g, i=i: e.tensor_copy(
                            out=h_state[:, j * NS + g * SG + i:
                                        j * NS + g * SG + i + 1],
                            in_=h_traj[:, i * LC + LC - 1:i * LC + LC]), eDVE)
                    op("vector", lambda e: e.tensor_add(
                        out=hcb[:, 0:LC], in0=hcb[:, 0:LC], in1=hcb[:, LC:2 * LC]),
                        eDVE)
                    op("vector", lambda e: e.tensor_add(
                        out=hcb[:, 2 * LC:3 * LC], in0=hcb[:, 2 * LC:3 * LC],
                        in1=hcb[:, 3 * LC:4 * LC]), eDVE)
                    if g == 0:
                        op("vector", lambda e, j=j: e.tensor_add(
                            out=y_buf[:, j * LC:(j + 1) * LC],
                            in0=hcb[:, 0:LC], in1=hcb[:, 2 * LC:3 * LC]), eDVE)
                    else:
                        op("vector", lambda e: e.tensor_add(
                            out=hc[:, 0:LC], in0=hcb[:, 0:LC],
                            in1=hcb[:, 2 * LC:3 * LC]), eDVE)
                        op("vector", lambda e, j=j: e.tensor_add(
                            out=y_buf[:, j * LC:(j + 1) * LC],
                            in0=y_buf[:, j * LC:(j + 1) * LC], in1=hc[:, 0:LC]), eDVE)
            for g in range(4):
                stages.append(lambda g=g: s_scan(g, 0))
                stages.append(lambda g=g: s_scan(g, 1))

            def s_y():
                for j in range(NJ):
                    op("vector", lambda e, j=j: e.scalar_tensor_tensor(
                        out=y_buf[:, j * LC:(j + 1) * LC],
                        in0=x_ssm[:, j * LC:(j + 1) * LC],
                        scalar=D_fm[:, j:j + 1], in1=y_buf[:, j * LC:(j + 1) * LC],
                        op0=ALU.mult, op1=ALU.add), eDVE)
                    op("vector", lambda e, j=j: e.tensor_mul(
                        out=xn_b[:, j * LC:(j + 1) * LC],
                        in0=y_buf[:, j * LC:(j + 1) * LC],
                        in1=z_silu[:, j * LC:(j + 1) * LC]), eDVE)
                cs["v_yb"] = eDVE.v
            stages.append(s_y)

            def s_wout():
                for mt in range(8):
                    if c == 0 and mt == 0:
                        wait("sync", cCVwo, v_cv_wo)
                    u, sl = stream_front_unit(s_Wout[mt])
                    wait("tensor", fw_state["loaded"][u])
                    if mt == 0:
                        wait("tensor", eDVE, cs["v_yb"])
                    bank_i = 4 + (mt % 4)
                    bank = pb[bank_i]
                    bank_acquire("tensor", bank_i)
                    for ks in range(8):
                        op("tensor", lambda e, sl=sl, ks=ks, bank=bank: e.matmul(
                            bank[:, 0:LC], fw[:, sl + ks * 128: sl + (ks + 1) * 128],
                            xn_b[:, ks * LC:(ks + 1) * LC],
                            start=(ks == 0), stop=(ks == 7)), ePE, 1 if ks == 7 else None)
                    fw_state["consumed"][u] = ePE.v
                    v_mm = ePE.v
                    wait("vector", ePE, v_mm)
                    op("vector", lambda e, mt=mt, bank=bank: e.scalar_tensor_tensor(
                        out=x_fm[:, mt * LC:(mt + 1) * LC],
                        in0=x_fm[:, mt * LC:(mt + 1) * LC], scalar=2.0,
                        in1=bank[:, 0:LC], op0=ALU.mult, op1=ALU.add), eDVE)
                    bank_release(bank_i, eDVE, eDVE.v)
                cs["v_s1"] = eDVE.v
            stages.append(s_wout)

            def s_ln2():
                if (p_owner - 1) in pass_S:
                    wait("vector", eDVE, pass_S[p_owner - 1]["v_oinit"])

                def writer(j):
                    xo = j * SC + half * LC
                    op("vector", lambda e, j=j: e.tensor_sub(
                        out=hc[:, 0:LC], in0=x_fm[:, j * LC:(j + 1) * LC],
                        in1=pb[4][:, 0:LC]), eDVE)
                    op("vector", lambda e, xo=xo: e.tensor_mul(
                        out=x2[:, xo:xo + LC], in0=hc[:, 0:LC], in1=pb[5][:, 0:LC]),
                        eDVE)
                fm_ln_stats_apply(x_fm, LC, writer, cs["v_s1"])
                cs["v_x2"] = eDVE.v
            stages.append(s_ln2)

            def s_x2b():
                if (p_owner - 1) in pass_S:
                    ps_prev = pass_S[p_owner - 1]
                    assert "v_ln3stat" in ps_prev, "x2b before LN3 stats"
                    wait("vector", ePE, ps_prev["v_ln3stat"])
                for j in range(NJ):
                    xo = j * SC + half * LC
                    op("vector", lambda e, xo=xo: e.tensor_copy(
                        out=x2b[:, xo:xo + LC], in_=x2[:, xo:xo + LC]), eDVE)
                cs["v_x2b"] = eDVE.v
            return stages, s_x2b

        w1s = {"count": 0, "loaded": {}, "consumed": {}}
        w2s = {"count": 0, "loaded": {}, "consumed": {}}
        out_dma_v = {}

        def make_moe_pass(p):
            ps = pass_S.setdefault(p, {})
            ps["v_oadd"] = {}
            ps["ounit"] = [0]
            tok0 = p * SC
            w2rot = [2, 3, 0, 1]

            def b_gate():
                wait("tensor", eDVE, chunk_S[2 * p + 1]["v_x2"])
                bank_acquire("tensor", 0)
                for tt in range(4):
                    for ks in range(8):
                        op("tensor", lambda e, tt=tt, ks=ks: e.matmul(
                            pb[0][:, tt * NE:(tt + 1) * NE],
                            x2[:, ks * SC + tt * 128: ks * SC + tt * 128 + 128],
                            Wg_sb[:, ks * NE:(ks + 1) * NE],
                            start=(ks == 0), stop=(ks == 7)),
                            ePE, 1 if (tt == 3 and ks == 7) else None)
                v_gmm = ePE.v
                wait("vector", ePE, v_gmm)
                fence_dve()
                for tt in range(4):
                    op("vector", lambda e, tt=tt: e.tensor_add(
                        out=gate_tm[:, tt * NE:(tt + 1) * NE],
                        in0=pb[0][:, tt * NE:(tt + 1) * NE], in1=bg_bcast[:]), eDVE)
                bank_release(0, eDVE, eDVE.v)
                for tt in range(4):
                    op("vector", lambda e, tt=tt: e.reduce_max(
                        out=gsc[:, tt:tt + 1], in_=gate_tm[:, tt * NE:(tt + 1) * NE],
                        axis=AX.X), eDVE)
                fence_dve()
                op("vector", lambda e: e.tensor_scalar(
                    out=gsc[:, 0:4], in0=gsc[:, 0:4], scalar1=-1.0, scalar2=None,
                    op0=ALU.mult), eDVE)
                fence_dve()
                v_m = eDVE.v
                wait("scalar", eDVE, v_m)
                for tt in range(4):
                    go = tt * NE
                    op("scalar", lambda e, go=go, tt=tt: e.activation(
                        out=probs[:, go:go + NE], in_=gate_tm[:, go:go + NE],
                        func=AF.Exp, bias=gsc[:, tt:tt + 1]), eACT)
                fence_act()
                v_e = eACT.v
                wait("vector", eACT, v_e)
                for tt in range(4):
                    op("vector", lambda e, tt=tt: e.reduce_sum(
                        out=gsc[:, 4 + tt:5 + tt],
                        in_=probs[:, tt * NE:(tt + 1) * NE], axis=AX.X), eDVE)
                fence_dve()
                op("vector", lambda e: e.reciprocal(out=gsc[:, 4:8], in_=gsc[:, 4:8]),
                   eDVE)
                fence_dve()
                for tt in range(4):
                    go = tt * NE
                    op("vector", lambda e, go=go, tt=tt: e.tensor_scalar(
                        out=probs[:, go:go + NE], in0=probs[:, go:go + NE],
                        scalar1=gsc[:, 4 + tt:5 + tt], scalar2=None, op0=ALU.mult),
                        eDVE)
                fence_dve()
                for tt in range(4):
                    op("vector", lambda e, tt=tt: e.reduce_max(
                        out=w_tm[:, tt:tt + 1], in_=probs[:, tt * NE:(tt + 1) * NE],
                        axis=AX.X), eDVE)
                fence_dve()
                for tt in range(4):
                    go = tt * NE
                    op("vector", lambda e, go=go, tt=tt: e.tensor_scalar(
                        out=mask_tm[:, go:go + NE], in0=probs[:, go:go + NE],
                        scalar1=w_tm[:, tt:tt + 1], scalar2=None, op0=ALU.is_ge), eDVE)
                fence_dve()
                for tt in range(4):
                    go = tt * NE
                    op("vector", lambda e, go=go, tt=tt: e.tensor_scalar(
                        out=mask_tm[:, go:go + NE], in0=mask_tm[:, go:go + NE],
                        scalar1=w_tm[:, tt:tt + 1], scalar2=None, op0=ALU.mult), eDVE)
                fence_dve()
                v_mask = eDVE.v
                if dbg and p == 0:
                    dbg_dump("probs", probs[:], [128, 4 * NE], F32, eDVE, v_mask)
                    dbg_dump("maskwtm", mask_tm[:], [128, 4 * NE], F32, eDVE, v_mask)
                wait("tensor", eDVE, v_mask)
                bank_acquire("tensor", 1)
                for tt in range(4):
                    op("tensor", lambda e, tt=tt: e.transpose(
                        pb[1][0:NE, tt * 128:(tt + 1) * 128],
                        mask_tm[:, tt * NE:(tt + 1) * NE], ident_f[:]),
                        ePE, 1 if tt == 3 else None)
                v_maskT = ePE.v
                wait("vector", ePE, v_maskT)
                fence_dve()
                op("vector", lambda e: e.tensor_copy(out=maskw_fm_b[:],
                                                     in_=pb[1][0:NE, 0:SC]), eDVE)
                bank_release(1, eDVE, eDVE.v)
                fence_dve()
                v_maskf = eDVE.v
                wait("tensor", eDVE, v_maskf)
                for dt in range(8):
                    bank_i = dt % 2
                    bank = pb[bank_i]
                    bank_acquire("tensor", bank_i)
                    op("tensor", lambda e, dt=dt, bank=bank: e.matmul(
                        bank[:, 0:SC], b2_b[0:NE, dt * 128:(dt + 1) * 128],
                        maskw_fm_b[0:NE, :], start=True, stop=True), ePE, 1)
                    v_b2 = ePE.v
                    wait("vector", ePE, v_b2)
                    op("vector", lambda e, dt=dt, bank=bank: e.tensor_add(
                        out=o_acc[:, dt * SC:(dt + 1) * SC],
                        in0=x2[:, dt * SC:(dt + 1) * SC], in1=bank[:, 0:SC]), eDVE, 1)
                    bank_release(bank_i, eDVE, eDVE.v)
                ps["v_oinit"] = eDVE.v

            def b_w1(ee, huh):
                if huh == 0:
                    bank_acquire("tensor", 2)
                    op("tensor", lambda e, ee=ee: e.matmul(
                        pb[2][:, 0:SC], sel8_b[0:NE, ee * 128:(ee + 1) * 128],
                        maskw_fm_b[0:NE, :], start=True, stop=True), ePE, 1)
                    v_mb = ePE.v
                    wait("vector", ePE, v_mb)
                    op("vector", lambda e: e.tensor_copy(out=mask_b[:],
                                                         in_=pb[2][:, 0:SC]), eDVE)
                    bank_release(2, eDVE, eDVE.v)
                    ps["v_mbs"] = eDVE.v
                for hu in range(huh * 8, huh * 8 + 8):
                    u = w1s["count"]
                    if u >= 3:
                        wait("sync", ePE, w1s["consumed"][u - 3])
                    sl = (u % 3) * 2048
                    if p == 0 and hu == 0 and huh == 0:
                        wait("sync", v_cv_e[ee])
                    c_ = cW1s[u % 3]
                    v = op("sync", lambda e, ee=ee, hu=hu, sl=sl: e.dma_start(
                        out=w1r[:, sl:sl + 2048], in_=s_W1[ee, hu]), c_, 16)
                    w1s["loaded"][u] = (c_.sem, v)
                    w1s["count"] = u + 1
                    wait("tensor", w1s["loaded"][u])
                    if hu == 0 and huh == 0:
                        wait("tensor", eDVE, chunk_S[2 * p + 1]["v_x2b"])
                    for ht in range(2):
                        hidx = hu * 2 + ht
                        bank_i = hidx % 4
                        bank = pb[bank_i]
                        bank_acquire("tensor", bank_i)
                        for ks in range(8):
                            op("tensor", lambda e, sl=sl, ks=ks, ht=ht, bank=bank:
                               e.matmul(
                                   bank[:, 0:SC],
                                   w1r[:, sl + ks * 256 + ht * 128:
                                       sl + ks * 256 + ht * 128 + 128],
                                   x2b[:, ks * SC:(ks + 1) * SC],
                                   start=(ks == 0), stop=(ks == 7)),
                               ePE, 1 if ks == 7 else None)
                        v_mm = ePE.v
                        if ht == 1:
                            w1s["consumed"][u] = v_mm
                        if ee == NE - 1 and hidx == 31:
                            ps["v_w1_last"] = v_mm
                        wait("scalar", ePE, v_mm)
                        v_silu = op(
                            "scalar", lambda e, ee=ee, hidx=hidx, bank=bank:
                            e.activation(
                                out=h1[:, hidx * SC:(hidx + 1) * SC],
                                in_=bank[:, 0:SC], func=AF.Silu,
                                bias=b1_fm[:, ee * 32 + hidx:ee * 32 + hidx + 1]),
                            eACT, 1)
                        bank_release(bank_i, eACT, v_silu)
                if huh == 1:
                    ps["v_h1"] = eACT.v

            def b_w2(ee, dth):
                for dt in range(dth * 4, dth * 4 + 4):
                    if p == 0 and dt == 0 and dth == 0:
                        wait("sync", v_cv_e[ee])
                    gi = ps["ounit"][0]
                    bank_i = [2, 3, 0, 1][gi % 4]
                    bank = pb[bank_i]
                    bank_acquire("tensor", bank_i)
                    for hh in range(2):
                        u = w2s["count"]
                        if u >= 3:
                            wait("sync", ePE, w2s["consumed"][u - 3])
                        sl = (u % 3) * 2048
                        c_ = cW2s[u % 3]
                        v = op("sync", lambda e, ee=ee, dt=dt, hh=hh, sl=sl:
                               e.dma_start(
                                   out=w2r[:, sl:sl + 2048],
                                   in_=s_W2[ee, dt][:, hh * 2048:(hh + 1) * 2048]),
                               c_, 16)
                        w2s["loaded"][u] = (c_.sem, v)
                        w2s["count"] = u + 1
                        wait("tensor", w2s["loaded"][u])
                        if dt == 0 and dth == 0 and hh == 0:
                            wait("tensor", eACT, ps["v_h1"])
                        for hs in range(16):
                            hsg = hh * 16 + hs
                            op("tensor", lambda e, sl=sl, hs=hs, hsg=hsg, bank=bank:
                               e.matmul(
                                   bank[:, 0:SC],
                                   w2r[:, sl + hs * 128:sl + (hs + 1) * 128],
                                   h1[:, hsg * SC:(hsg + 1) * SC],
                                   start=(hsg == 0), stop=(hsg == 31)),
                               ePE, 1 if hs == 15 else None)
                        w2s["consumed"][u] = ePE.v
                    v_mm = ePE.v
                    wait("vector", ePE, v_mm)
                    if ee == 0 and dt == 0:
                        wait("vector", eDVE, ps["v_mbs"])
                    op("vector", lambda e, bank=bank: e.tensor_mul(
                        out=tmp_o[:], in0=bank[:, 0:SC], in1=mask_b[:]), eDVE)
                    vv = op("vector", lambda e, dt=dt: e.tensor_add(
                        out=o_acc[:, dt * SC:(dt + 1) * SC],
                        in0=o_acc[:, dt * SC:(dt + 1) * SC], in1=tmp_o[:]), eDVE, 1)
                    ps["v_oadd"][gi] = vv
                    bank_release(bank_i, eDVE, vv)
                    ps["ounit"][0] += 1

            def b_ln3out():
                v_s2 = ps["v_oadd"][ps["ounit"][0] - 1]
                if dbg and p == 0:
                    dbg_dump("oacc", o_acc[:], [128, NJ * SC], F32, eDVE, v_s2)
                op("vector", lambda e: e.tensor_copy(out=x2b[:], in_=o_acc[:]), eDVE)
                v_c1 = eDVE.v
                wait("scalar", eDVE, v_s2)
                op("scalar", lambda e: e.activation(out=h1[:, 0:NJ * SC], in_=o_acc[:],
                                                    func=AF.Square), eACT)
                v_c2 = eACT.v
                wait("tensor", eDVE, v_c1)
                wait("tensor", eACT, v_c2)
                bank_acquire("tensor", 0)
                bank_acquire("tensor", 1)
                for j in range(NJ):
                    op("tensor", lambda e, j=j: e.matmul(
                        pb[0][0:1, 0:SC], ones_col_b[:], x2b[:, j * SC:(j + 1) * SC],
                        start=(j == 0), stop=(j == NJ - 1)),
                        ePE, 1 if j == NJ - 1 else None)
                for j in range(NJ):
                    op("tensor", lambda e, j=j: e.matmul(
                        pb[1][0:1, 0:SC], ones_col_b[:], h1[:, j * SC:(j + 1) * SC],
                        start=(j == 0), stop=(j == NJ - 1)),
                        ePE, 1 if j == NJ - 1 else None)
                v_stat = ePE.v
                ps["v_ln3stat"] = v_stat
                wait("vector", ePE, v_stat)
                fence_dve()
                op("vector", lambda e: e.tensor_scalar(out=mu_row[0:1, 0:SC],
                                                       in0=pb[0][0:1, 0:SC],
                                                       scalar1=1.0 / D, scalar2=None,
                                                       op0=ALU.mult), eDVE)
                op("vector", lambda e: e.tensor_mul(out=tmp_row[0:1, 0:SC],
                                                    in0=mu_row[0:1, 0:SC],
                                                    in1=mu_row[0:1, 0:SC]), eDVE)
                op("vector", lambda e: e.scalar_tensor_tensor(
                    out=var_row[0:1, 0:SC], in0=pb[1][0:1, 0:SC], scalar=1.0 / D,
                    in1=tmp_row[0:1, 0:SC], op0=ALU.mult, op1=ALU.subtract), eDVE)
                op("vector", lambda e: e.tensor_scalar(
                    out=var_row[0:1, 0:SC], in0=var_row[0:1, 0:SC], scalar1=LN_EPS,
                    scalar2=None, op0=ALU.add), eDVE)
                fence_dve()
                v_var = eDVE.v
                wait("scalar", eDVE, v_var)
                op("scalar", lambda e: e.activation(out=tmp_row[0:1, 0:SC],
                                                    in_=var_row[0:1, 0:SC],
                                                    func=AF.Ln), eACT)
                op("scalar", lambda e: e.activation(out=rstd_row[0:1, 0:SC],
                                                    in_=tmp_row[0:1, 0:SC],
                                                    func=AF.Exp, scale=-0.5), eACT)
                fence_act()
                v_rstd = eACT.v
                wait("tensor", eACT, v_rstd)
                op("tensor", lambda e: e.matmul(pb[0][:, 0:SC], ones_row_f[0:1, :],
                                                mu_row[0:1, 0:SC],
                                                start=True, stop=True), ePE)
                op("tensor", lambda e: e.matmul(pb[1][:, 0:SC], ones_row_f[0:1, :],
                                                rstd_row[0:1, 0:SC],
                                                start=True, stop=True), ePE, 1)
                v_bcst = ePE.v
                wait("vector", ePE, v_bcst)
                for j in range(NJ):
                    op("vector", lambda e, j=j: e.tensor_sub(
                        out=tmp_o[:], in0=o_acc[:, j * SC:(j + 1) * SC],
                        in1=pb[0][:, 0:SC]), eDVE)
                    op("vector", lambda e, j=j: e.tensor_mul(
                        out=o_acc[:, j * SC:(j + 1) * SC], in0=tmp_o[:],
                        in1=pb[1][:, 0:SC]), eDVE)
                bank_release(0, eDVE, eDVE.v)
                bank_release(1, eDVE, eDVE.v)
                v_x3 = eDVE.v
                wait("tensor", eDVE, v_x3)
                for tt in range(4):
                    bank_acquire("tensor", 0)
                    bank_acquire("tensor", 1)
                    for j in range(NJ):
                        bank = pb[0] if j < 4 else pb[1]
                        colo = (j % 4) * 128
                        op("tensor", lambda e, j=j, tt=tt, bank=bank, colo=colo:
                           e.transpose(
                               bank[:, colo:colo + 128],
                               o_acc[:, j * SC + tt * 128: j * SC + tt * 128 + 128],
                               ident_f[:]), ePE, 1 if j == NJ - 1 else None)
                    v_trT = ePE.v
                    wait("vector", ePE, v_trT)
                    prev = (p, tt - 1) if tt > 0 else (p - 1, 3)
                    if prev in out_dma_v:
                        wait("vector", cOUT, out_dma_v[prev])
                    for bi, bank in ((0, pb[0]), (1, pb[1])):
                        op("vector", lambda e, bi=bi, bank=bank: e.tensor_copy(
                            out=out_tm[:, bi * 512:(bi + 1) * 512],
                            in_=bank[:, 0:512]), eDVE, 1)
                    v_evT = eDVE.v
                    bank_release(0, eDVE, v_evT)
                    bank_release(1, eDVE, v_evT)
                    wait("gpsimd", eDVE, v_evT)
                    out_dma_v[(p, tt)] = op("gpsimd", lambda e, tt=tt, tok0=tok0:
                                            e.dma_start(
                                                out=d_out[tok0 + tt * 128:
                                                          tok0 + (tt + 1) * 128, :],
                                                in_=out_tm[:]), cOUT, 16)

            expert_blocks = []
            for ee in range(NE):
                expert_blocks.append(lambda ee=ee: b_w1(ee, 0))
                expert_blocks.append(lambda ee=ee: b_w1(ee, 1))
                expert_blocks.append(lambda ee=ee: b_w2(ee, 0))
                expert_blocks.append(lambda ee=ee: b_w2(ee, 1))
            return b_gate, expert_blocks, b_ln3out

        for c in (0, 1):
            if c < nchunk:
                stages, x2b_stage = make_front_stages(c)
                for stg in stages:
                    stg()
                x2b_stage()
        for p in range(npass):
            b_gate, expert_blocks, b_ln3out = make_moe_pass(p)
            fstages = []
            x2b_stages = []
            for cc in (2 * p + 2, 2 * p + 3):
                if cc < nchunk:
                    stages, x2b_stage = make_front_stages(cc)
                    fstages += stages
                    x2b_stages.append(x2b_stage)
            b_gate()
            nb = len(expert_blocks)
            nf = len(fstages)
            fi = 0
            for bi, blk_fn in enumerate(expert_blocks):
                blk_fn()
                want = ((bi + 1) * nf) // nb
                while fi < want:
                    fstages[fi]()
                    fi += 1
            while fi < nf:
                fstages[fi]()
                fi += 1
            b_ln3out()
            for stg in x2b_stages:
                stg()

        wait("gpsimd", cOUT, cOUT.v)
        if dbg:
            wait("gpsimd", cDBG, cDBG.v)

        counts = {k: len(v) for k, v in prog.items()}

        def runner(name):
            def body(eng):
                for fn in prog[name]:
                    fn(eng)
            return body

        blk = st.enter_context(nc.Block())
        blk.sync(runner("sync"))
        blk.tensor(runner("tensor"))
        blk.vector(runner("vector"))
        blk.scalar(runner("scalar"))
        blk.gpsimd(runner("gpsimd"))

    nc._instr_counts = counts
    return nc


def host_inputs(x_core, w):
    def fm(v):
        return np.ascontiguousarray(np.asarray(v, np.float32).reshape(NJ, 128).T)

    sel16 = np.zeros((NS, NS * 128), np.float32)
    for s in range(NS):
        sel16[s, s * 128:(s + 1) * 128] = 1.0
    sel8 = np.zeros((NE, NE * 128), np.float32)
    for s in range(NE):
        sel8[s, s * 128:(s + 1) * 128] = 1.0
    b1 = np.asarray(w["L0_ffn_b1"], np.float32)
    b1_fm = np.ascontiguousarray(
        b1.reshape(NE, 32, 128).transpose(2, 0, 1).reshape(128, NE * 32))
    conv_w = np.asarray(w["L0_attn_conv_w"], np.float32)[:, 0, :]
    conv_w_fm = np.ascontiguousarray(
        conv_w.reshape(NJ, 128, KCONV).transpose(1, 0, 2).reshape(128, NJ * KCONV))
    f32 = lambda k: np.ascontiguousarray(np.asarray(w[k], np.float32))
    return dict(
        x=np.ascontiguousarray(np.asarray(x_core, np.float32)),
        ln_g_fm=fm(w["L0_attn_ln_g"]), ln_b_fm=fm(w["L0_attn_ln_b"]),
        W_in=f32("L0_attn_W_in"), conv_w_fm=conv_w_fm,
        conv_b_fm=fm(w["L0_attn_conv_b"]),
        dt_w=f32("L0_attn_dt_w"), dt_b_fm=fm(w["L0_attn_dt_b"]),
        B_w=f32("L0_attn_B_w"), C_w=f32("L0_attn_C_w"),
        log_A_fm=fm(w["L0_attn_log_A"]), D_fm=fm(w["L0_attn_D_param"]),
        W_out=f32("L0_attn_W_out"), Wg=f32("L0_ffn_Wg"),
        bg_row=np.ascontiguousarray(np.asarray(w["L0_ffn_bg"], np.float32)
                                    .reshape(1, NE)),
        W1=f32("L0_ffn_W1"), b1_fm=b1_fm, W2=f32("L0_ffn_W2"), b2=f32("L0_ffn_b2"),
        ident=np.eye(128, dtype=np.float32),
        ones_col=np.ones((128, 1), np.float32),
        ones_row=np.ones((1, 128), np.float32),
        sel16=sel16, sel8=sel8,
    )


def kernel(**inputs):
    x = np.asarray(inputs["L0_attn_x"])
    B, T, _ = x.shape
    nc = build_nc(T, detect_races=False)
    in_maps = [host_inputs(x[i], inputs) for i in range(B)]
    res = run_bass_kernel_spmd(nc, in_maps, core_ids=list(range(B)))
    return np.stack([r["out"] for r in res.results], axis=0)
